# version 06
# baseline (speedup 1.0000x reference)
import numpy as np
import ml_dtypes
import concourse.bass as bass
import concourse.mybir as mybir
from concourse.bass_utils import run_bass_kernel_spmd

F32 = mybir.dt.float32
BF16 = mybir.dt.bfloat16
AF = mybir.ActivationFunctionType
ALU = mybir.AluOpType
BF16NP = ml_dtypes.bfloat16

A_FH = [7, 4, 3, 0]
A_SH = [8, 11, 12, 15]
B_FH = [6, 5, 2, 1]
B_SH = [9, 10, 13, 14]
POS_BLOCKS = {0: A_FH + A_SH, 1: B_FH + B_SH}
A_SET = set(A_FH + A_SH)
FH_W = [8, 6, 4, 2]
SH_KS = [0, 2, 4, 6]
SH_NCH = 10
NEG = -30000.0
QSCALE = float(1.0 / np.sqrt(128.0))
EPS_OUTER = 1e-5
EPS_FFN = 1e-6

NPOS, NH, NKV, HS, C, TOK = 8, 16, 4, 128, 2048, 1024
D1N, AGN = 1024, 256

OWNER = {}
for _g in range(16):
    _par = 0 if _g in A_SET else 1
    OWNER[_g] = (_par, POS_BLOCKS[_par].index(_g))

ENGS = ("sync", "gpsimd", "scalar", "vector", "tensor")


def _mask_pack(par):
    i_ = np.arange(128)[:, None]
    j_ = np.arange(128)[None, :]
    tri_diag = np.where(j_ <= i_, 0.0, NEG)
    tri_edge = np.where(j_ > i_, 0.0, NEG)
    full = np.full((128, 128), NEG)
    zero = np.zeros((128, 128))
    tiles = []
    for i in range(4):
        Bq = POS_BLOCKS[par][i]
        W = FH_W[i]
        for cch in (W - 2, W - 1):
            tiles.append(tri_diag if cch == Bq else (full if cch > Bq else zero))
    for i in range(4):
        Bq = POS_BLOCKS[par][4 + i]
        ks = SH_KS[i]
        cbs = (Bq - 8) - ks
        cd = Bq - ks
        tiles.append(tri_edge if cbs == 0 else full)
        tiles.append(tri_edge if cbs == 1 else zero)
        tiles.append(tri_diag if cd == 8 else zero)
        tiles.append(tri_diag if cd == 9 else full)
    sinks = []
    for i in range(4):
        Bq = POS_BLOCKS[par][4 + i]
        if Bq == 8:
            sm = np.where(np.arange(4)[None, :] <= i_, 0.0, NEG)
        else:
            sm = np.zeros((128, 4))
        sinks.append(sm)
    arr = np.concatenate(tiles + sinks, axis=1)
    return np.ascontiguousarray(arr.astype(BF16NP))


def _mblocks(w_t, nblk):
    a = w_t.reshape(16, 128, nblk, 128).transpose(2, 1, 0, 3).reshape(nblk, 128, 2048)
    return np.ascontiguousarray(a)


def _osubs(w, nsub, sub):
    a = w.reshape(16, 128, nsub, sub).transpose(2, 1, 0, 3).reshape(nsub, 128, 16 * sub)
    return np.ascontiguousarray(a)


def make_core_inputs(c, x, Wq, Wk, Wv, Wo, in_g, in_b, agg_w, agg_b, pn_g, pn_b,
                     d1_w, d1_b, d2_w, d2_b, bc_w, bc_b, gate_w, gate_b,
                     out_g, out_b, shared):
    b, par = c // 2, c % 2
    blocks = POS_BLOCKS[par]
    xb = x[b].reshape(16, 128, C)
    x_own = np.ascontiguousarray(xb[blocks].reshape(TOK, C))
    one_hot = np.zeros((4, 128), BF16NP)
    one_hot[b, :] = BF16NP(1.0)
    d = dict(shared)
    d.update({
        "x_own": x_own,
        "masks": _mask_pack(par),
        "b1sel": one_hot,
        "agg_sl": np.ascontiguousarray(agg_w[:, c * AGN:(c + 1) * AGN].reshape(16, 128, AGN)),
        "bc_sl": np.ascontiguousarray(bc_w[:, c * AGN:(c + 1) * AGN].reshape(16, 128, AGN)),
        "d1_sl": np.ascontiguousarray(d1_w[:, c * D1N:(c + 1) * D1N].reshape(16, 128, D1N)),
        "d2_sl": np.ascontiguousarray(d2_w[c * D1N:(c + 1) * D1N, :].reshape(8, 128, C)),
        "d1_b": np.ascontiguousarray(d1_b[c * D1N:(c + 1) * D1N].reshape(1, D1N)),
        "agg_b": np.ascontiguousarray(agg_b[c * AGN:(c + 1) * AGN].reshape(1, AGN)),
        "bc_b": np.ascontiguousarray(bc_b[c * AGN:(c + 1) * AGN].reshape(1, AGN)),
    })
    return d


class Plan:
    def __init__(self, nc):
        self.nc = nc
        self.q = {e: [] for e in ENGS}
        self.cnt = {}
        self.sem = {}

    def new_sem(self, name):
        self.sem[name] = self.nc.alloc_semaphore(name)
        self.cnt[name] = 0

    def op(self, eng, fn, waits=(), incs=()):
        resolved = []
        for sname, amt in incs:
            if amt is None:
                self.cnt[sname] += 1
            else:
                self.cnt[sname] += amt
            resolved.append((self.sem[sname], amt))
        wl = [(self.sem[s], v) for s, v in waits]
        self.q[eng].append((wl, fn, resolved))

    def dma(self, eng, out_ap, in_ap, waits=(), sem="dq"):
        def fn(e, o=out_ap, i=in_ap):
            return e.dma_start(out=o, in_=i)
        self.op(eng, fn, waits=waits, incs=((sem, 16),))
        return self.cnt[sem]

    def replay(self, eng, e):
        for wl, fn, incs in self.q[eng]:
            for h, v in wl:
                e.wait_ge(h, v)
            ins = fn(e)
            for h, amt in incs:
                assert ins is not None, f"{eng}: inc on op that returned None"
                ins = ins.then_inc(h) if amt is None else ins.then_inc(h, amt)


def s_layout(pos):
    if pos < 4:
        W = FH_W[pos]
        wc = W * 128
        chunks, c0 = [], 0
        while c0 < wc:
            n = min(512, wc - c0)
            chunks.append((c0, n, c0))
            c0 += n
        mops = [(0, 128, (W - 2) * 128), (128, 128, (W - 1) * 128)]
        return wc, chunks, mops, W, 0, False
    j = pos - 4
    ks = SH_KS[j]
    wc = SH_NCH * 128 + 4
    chunks = [(0, 512, ks * 128), (512, 512, ks * 128 + 512), (1024, 256, ks * 128 + 1024)]
    mops = [(0, 128, 0), (128, 128, 128), (256, 128, 8 * 128), (384, 128, 9 * 128)]
    return wc, chunks, mops, SH_NCH, ks, True


def build_program(debug=False, upto=6):
    nc = bass.Bass("TRN2", target_bir_lowering=False, debug=False)
    P = Plan(nc)
    for s in ("dq", "dg", "tv", "ts", "tt", "tg", "c_kv", "c_agg", "c_rs", "c_bc"):
        P.new_sem(s)
    S = P.sem
    C_ = P.cnt

    def inp(name, shape, dt=F32):
        return nc.dram_tensor(name, shape, dt, kind="ExternalInput").ap()

    x_own = inp("x_own", [TOK, C])
    x_last = inp("x_last", [4, C])
    wq_t = inp("wq_t", [16, 128, 2048])
    wk_t = inp("wk_t", [4, 128, 2048])
    wv_t = inp("wv_t", [4, 128, 2048])
    wo_t = inp("wo_t", [8, 128, 16 * 256])
    gw_t = inp("gw_t", [8, 128, 16 * 256])
    agg_sl = inp("agg_sl", [16, 128, AGN])
    bc_sl = inp("bc_sl", [16, 128, AGN])
    d1_sl = inp("d1_sl", [16, 128, D1N])
    d2_sl = inp("d2_sl", [8, 128, C])
    masks = inp("masks", [128, 3088], BF16)
    b1sel = inp("b1sel", [4, 128], BF16)
    gate_b = inp("gate_b", [1, C])
    d1_b = inp("d1_b", [1, D1N])
    agg_b = inp("agg_b", [1, AGN])
    bc_b = inp("bc_b", [1, AGN])
    d2_b = inp("d2_b", [1, C])
    out_g = inp("out_g", [1, C])
    out_be = inp("out_be", [1, C])
    in_g = inp("in_g", [1, C])
    in_b = inp("in_b", [1, C])
    pn_g = inp("pn_g", [1, C])
    pn_b = inp("pn_b", [1, C])

    out_own = nc.dram_tensor("out_own", [TOK, C], F32, kind="ExternalOutput").ap()

    dbg = {}
    if debug:
        for name, shape, dt in [
            ("dbg_xnT", [128, 16 * TOK], BF16), ("dbg_QT", [128, 16384], BF16),
            ("dbg_KV", [128, 16384], BF16), ("dbg_yT", [128, 16384], BF16),
            ("dbg_s", [128, 16384], BF16), ("dbg_th", [4, 2048], F32),
            ("dbg_steps", [4, 6 * 2048], F32), ("dbg_xl", [128, 128], BF16),
            ("dbg_tn", [4, 2048], BF16), ("dbg_g1", [4, 1024], BF16),
            ("dbg_h1", [4, 1024], F32), ("dbg_us", [128, 32], F32),
            ("dbg_part", [4, 2048], F32), ("dbg_red", [4, 2048], F32),
            ("dbg_bc", [128, 2048], F32), ("dbg_u", [128, 16384], BF16),
        ]:
            dbg[name] = nc.dram_tensor(name, shape, dt, kind="ExternalOutput").ap()

    kv_in = nc.dram_tensor("kv_in", [1024, TOK], BF16, kind="Internal").ap()
    kv_out = nc.dram_tensor("kv_out", [2048, TOK], BF16, kind="Internal").ap()
    d1_bfd = nc.dram_tensor("d1_bfd", [16, 128, D1N], BF16, kind="Internal").ap()
    d2_bfd = nc.dram_tensor("d2_bfd", [8, 128, C], BF16, kind="Internal").ap()
    agg_bfd = nc.dram_tensor("agg_bfd", [16, 128, AGN], BF16, kind="Internal").ap()
    bc_bfd = nc.dram_tensor("bc_bfd", [16, 128, AGN], BF16, kind="Internal").ap()
    agg_ci = nc.dram_tensor("agg_ci", [4, AGN], F32, kind="Internal").ap()
    agg_co = nc.dram_tensor("agg_co", [32, AGN], F32, kind="Internal").ap()
    rs_ci = nc.dram_tensor("rs_ci", [4, C], F32, kind="Internal").ap()
    rs_co = nc.dram_tensor("rs_co", [4, C], F32, kind="Internal").ap()
    bc_ci = nc.dram_tensor("bc_ci", [4, AGN], F32, kind="Internal").ap()
    bc_co = nc.dram_tensor("bc_co", [32, AGN], F32, kind="Internal").ap()

    eye = nc.inline_tensor(np.eye(128, dtype=BF16NP), name="eye128").ap()

    A = nc.alloc_sbuf_tensor
    xnT = A("xnT", [128, 16 * TOK], BF16).ap()
    A1 = A("A1", [128, 16384], BF16).ap()
    A2 = A("A2", [128, 16384], BF16).ap()
    YT = A("YT", [128, 16384], BF16).ap()
    STG = A("STG", [128, 8192], BF16).ap()
    TE = A("TE", [128, 4096], F32).ap()
    SM = A("SM", [128, 10560], F32).ap()

    stXa = STG[:, 0:4096].bitcast(F32)
    stXb = STG[:, 4096:8192].bitcast(F32)
    wbufa = STG[:, 0:4096]
    wbufb = STG[:, 4096:8192]
    th_a = STG[:, 0:2048]
    th_b = STG[:, 2048:4096]
    vt_f = STG[:, 0:4096].bitcast(F32)
    vt_b = STG[:, 4096:6144]

    thought = TE[0:4, 0:2048]
    tmp4 = TE[0:4, 2048:4096]
    bc_bcast = TE[:, 0:2048]
    g_bcast = TE[:, 2048:3072].bitcast(BF16)
    b_bcast = TE[:, 3072:4096].bitcast(BF16)

    _off = [0]
    def carve(n):
        ap = SM[:, _off[0]:_off[0] + n]
        _off[0] += n
        return ap
    P_sb = carve(656).bitcast(BF16)
    PT0 = carve(768).bitcast(BF16)
    PT1 = carve(768).bitcast(BF16)
    mk0 = carve(264).bitcast(BF16)
    mk1 = carve(264).bitcast(BF16)
    eye_sb = carve(64).bitcast(BF16)
    ones_sb = carve(64).bitcast(BF16)
    b1_sb = carve(64).bitcast(BF16)[0:4, :]
    rowsum = carve(8)
    recip = carve(8)
    bst = carve(32)
    bag = carve(8)
    nmu = carve(8)
    rsd = carve(8)
    une = carve(8)
    urs = carve(8)
    usum = carve(8)
    usq = carve(8)
    pacc1 = carve(8)
    pacc2 = carve(8)
    sqb = carve(256)
    thv = carve(1024)
    tn_bf = thv.bitcast(BF16)[0:4, :]
    d2b8_row = thv.bitcast(BF16)[32:33, :]
    png4 = carve(1024).bitcast(BF16)[0:4, :]
    pnb4 = carve(1024).bitcast(BF16)[0:4, :]
    g1_reg = carve(512)
    g1_bf = g1_reg.bitcast(BF16)[0:4, :]
    d1b_row = g1_reg.bitcast(BF16)[32:33, :]
    xlT = carve(64).bitcast(BF16)
    tnT = carve(64).bitcast(BF16)
    g1T = carve(32).bitcast(BF16)
    xnbuf = carve(1024).bitcast(BF16)
    vtmp = carve(256).bitcast(BF16)
    utmpa = carve(256)
    utmpb = carve(256)
    gt_reg = carve(128)
    gtmpa = gt_reg.bitcast(BF16)
    aggb_row = gt_reg.bitcast(BF16)[32:33, 0:256]
    gt_reg2 = carve(128)
    gtmpb = gt_reg2.bitcast(BF16)
    bcb_row = gt_reg2.bitcast(BF16)[32:33, 0:256]
    rows3 = carve(1024).bitcast(BF16)
    gateb_row = rows3[32:33, :]
    outg_row = rows3[64:65, :]
    outb_row = rows3[0:1, :]
    assert _off[0] <= 10560, _off[0]

    xre0 = xnT[:, 0:4096].bitcast(F32)
    xre1 = xnT[:, 4096:8192].bitcast(F32)

    PS0 = nc.alloc_psum_tensor("PS0", [128, 1536], F32).ap()
    PSPT = nc.alloc_psum_tensor("PSPT", [128, 2048], BF16).ap()
    PSY = nc.alloc_psum_tensor("PSY", [128, 1024], F32).ap()
    PST = nc.alloc_psum_tensor("PST", [128, 512], F32).ap()

    def t_op(fn, waits=(), inc=False):
        P.op("tensor", fn, waits=waits, incs=((("tt", 1),) if inc else ()))
        return C_["tt"]

    def v_op(fn, waits=(), inc=False):
        P.op("vector", fn, waits=waits, incs=((("tv", 1),) if inc else ()))
        return C_["tv"]

    def s_op(fn, waits=(), inc=False):
        P.op("scalar", fn, waits=waits, incs=((("ts", 1),) if inc else ()))
        return C_["ts"]

    def s_opd(fn, waits=(), inc=True):
        def wrap(e, fn=fn):
            fn(e)
            return e.drain()
        P.op("scalar", wrap, waits=waits, incs=((("ts", 1),) if inc else ()))
        return C_["ts"]

    def coll(kind, op, groups, ins, outs, wait_dg, sem):
        def fn(e, kind=kind, op=op, groups=groups, ins=ins, outs=outs):
            return e.collective_compute(kind, op, replica_groups=groups,
                                        ins=ins, outs=outs)
        P.op("gpsimd", fn, waits=[("dg", wait_dg)], incs=((sem, None),))
        return C_[sem]

    def ln_rows(src, dst, nrow, eps, junk=None, waits_first=()):
        if junk is None:
            junk = dst
        n = 2048.0
        ts1 = s_op(lambda e, src=src, junk=junk: e.activation(
            junk, src, AF.Identity, accum_out=bag[0:nrow, 0:1]),
            waits=waits_first, inc=True)
        tv1 = v_op(lambda e, nrow=nrow: e.tensor_scalar_mul(
            nmu[0:nrow, 0:1], bag[0:nrow, 0:1], -1.0 / n),
            waits=[("ts", ts1)], inc=True)
        ts2 = s_op(lambda e, src=src, junk=junk, nrow=nrow: e.activation(
            junk, src, AF.Square, bias=nmu[0:nrow, 0:1],
            accum_out=bag[0:nrow, 1:2]), waits=[("tv", tv1)], inc=True)
        tv2 = v_op(lambda e, nrow=nrow, eps=eps: e.tensor_scalar(
            rsd[0:nrow, 0:1], bag[0:nrow, 1:2], float(n * eps), 1.0 / n,
            ALU.add, ALU.mult), waits=[("ts", ts2)], inc=True)
        s_op(lambda e, nrow=nrow: (e.activation(
            rsd[0:nrow, 0:1], rsd[0:nrow, 0:1], AF.Ln), e.drain())[0],
            waits=[("tv", tv2)])
        ts_r = s_op(lambda e, nrow=nrow: e.activation(
            rsd[0:nrow, 0:1], rsd[0:nrow, 0:1], AF.Exp, scale=-0.5), inc=True)
        return v_op(lambda e, src=src, dst=dst, nrow=nrow: (e.drain(), e.tensor_scalar(
            dst, src, nmu[0:nrow, 0:1], rsd[0:nrow, 0:1], ALU.add, ALU.mult))[1],
            waits=[("ts", ts_r)], inc=True)

    P.dma("gpsimd", eye_sb[:, 0:128], eye, sem="dg")
    P.dma("gpsimd", b1_sb, b1sel, sem="dg")
    v_eye = C_["dg"]
    P.dma("gpsimd", d1_bfd, d1_sl, sem="dg")
    P.dma("gpsimd", d2_bfd, d2_sl, sem="dg")
    P.dma("gpsimd", agg_bfd, agg_sl, sem="dg")
    P.dma("gpsimd", bc_bfd, bc_sl, sem="dg")
    P.dma("gpsimd", gateb_row, gate_b, sem="dg")
    P.dma("gpsimd", outg_row, out_g, sem="dg")
    P.dma("gpsimd", outb_row, out_be, sem="dg")
    P.dma("gpsimd", d1b_row, d1_b, sem="dg")
    P.dma("gpsimd", aggb_row, agg_b, sem="dg")
    P.dma("gpsimd", bcb_row, bc_b, sem="dg")
    for r in range(4):
        P.dma("gpsimd", png4[r:r + 1, :], in_g, sem="dg")
        P.dma("gpsimd", pnb4[r:r + 1, :], in_b, sem="dg")
    v_ing = C_["dg"]
    v_d2b = P.dma("gpsimd", d2b8_row, d2_b, sem="dg")
    v_op(lambda e: e.tensor_scalar_mul(d2b8_row, d2b8_row, 0.125),
         waits=[("dg", v_d2b)])
    v_op(lambda e: e.memset(ones_sb[:, 0:128], 1.0), inc=True)
    v_xlast = P.dma("gpsimd", tmp4, x_last, sem="dg")

    tv_norm = {}
    ts_evac = {}
    tt_tr = {}
    for pos in range(NPOS):
        st = stXa if pos % 2 == 0 else stXb
        w = [("tv", tv_norm[pos - 2])] if pos >= 2 else []
        v_x = P.dma("sync", st, x_own[pos * 128:(pos + 1) * 128, :], waits=w)
        wv = [("dq", v_x)]
        if pos >= 1:
            wv.append(("tt", tt_tr[pos - 1]))
        ts1 = s_op(lambda e, st=st: e.activation(
            xnbuf[:, 0:2048], st, AF.Identity, accum_out=bag[:, 0:1]),
            waits=wv, inc=True)
        tv1 = v_op(lambda e: e.tensor_scalar_mul(nmu[:, 0:1], bag[:, 0:1],
                                                 -1.0 / 2048.0),
                   waits=[("ts", ts1)], inc=True)
        ts2 = s_op(lambda e, st=st: e.activation(
            xnbuf[:, 0:2048], st, AF.Square, bias=nmu[:, 0:1],
            accum_out=bag[:, 1:2]), waits=[("tv", tv1)], inc=True)
        tv2 = v_op(lambda e: e.tensor_scalar(
            rsd[:, 0:1], bag[:, 1:2], 2048.0 * EPS_OUTER, 1.0 / 2048.0,
            ALU.add, ALU.mult), waits=[("ts", ts2)], inc=True)
        s_op(lambda e: (e.activation(rsd[:, 0:1], rsd[:, 0:1], AF.Ln),
                        e.drain())[0], waits=[("tv", tv2)])
        ts_r = s_op(lambda e: e.activation(rsd[:, 0:1], rsd[:, 0:1], AF.Exp, scale=-0.5),
                    inc=True)
        tv_norm[pos] = v_op(lambda e, st=st: (e.drain(), e.tensor_scalar(
            xnbuf[:, 0:2048], st, nmu[:, 0:1], rsd[:, 0:1], ALU.add, ALU.mult))[1],
            waits=[("ts", ts_r)], inc=True)
        wt = [("tv", tv_norm[pos])]
        if pos == 0:
            wt.append(("dg", v_eye))
        else:
            wt.append(("ts", ts_evac[pos - 1]))
        def mk_tr(e):
            last = None
            for cch in range(16):
                last = e.transpose(PSPT[:, cch * 128:(cch + 1) * 128],
                                   xnbuf[:, cch * 128:(cch + 1) * 128],
                                   eye_sb[:, 0:128])
            return last
        tt_tr[pos] = t_op(mk_tr, waits=wt, inc=True)
        xnT3 = xnT.rearrange("p (c t) -> p c t", t=TOK)
        def mk_ev(e, pos=pos):
            last = None
            for half in range(2):
                src = PSPT[:, half * 1024:(half + 1) * 1024].rearrange(
                    "p (c q) -> p c q", q=128)
                dst = xnT3[:, half * 8:(half + 1) * 8, pos * 128:(pos + 1) * 128]
                last = e.copy(dst, src)
            return last
        ts_evac[pos] = s_op(mk_ev, waits=[("tt", tt_tr[pos])], inc=True)
    tv_p1 = C_["tv"]
    ts_p1 = C_["ts"]

    tv_l1 = ln_rows(tmp4, thought, 4, EPS_OUTER,
                    waits_first=[("dg", v_xlast)])
    ln_rows(thought, tn_bf, 4, EPS_FFN,
            waits_first=[("tv", tv_l1)])
    v_op(lambda e: e.tensor_mul(tn_bf, tn_bf, png4), waits=[("dg", v_ing)])
    tv_xln = v_op(lambda e: e.tensor_add(tn_bf, tn_bf, pnb4), inc=True)
    v_png = 0
    for r in range(4):
        P.dma("gpsimd", png4[r:r + 1, :], pn_g,
              waits=([("tv", tv_xln)] if r == 0 else ()), sem="dg")
        v_png = P.dma("gpsimd", pnb4[r:r + 1, :], pn_b, sem="dg")
    def mk_xlt(e):
        last = None
        for cch in range(16):
            last = e.transpose(PSPT[0:128, 1536 + cch * 4:1536 + (cch + 1) * 4],
                               tn_bf[:, cch * 128:(cch + 1) * 128], eye_sb[0:4, 0:4])
        return last
    tt_xlt = t_op(mk_xlt, waits=[("tv", tv_xln), ("ts", ts_evac[NPOS - 1])], inc=True)
    ts_xlt = s_opd(lambda e: e.copy(xlT[:, 0:64], PSPT[:, 1536:1600]),
                   waits=[("tt", tt_xlt)])
    if debug and upto <= 1:
        P.dma("gpsimd", dbg["dbg_xnT"], xnT[:, :], waits=[("ts", ts_p1)], sem="dg")
    if upto <= 1:
        return nc, P, dbg

    mlist = [("k", i, wk_t[i]) for i in range(4)] + \
            [("v", i, wv_t[i]) for i in range(4)] + \
            [("q", i, wq_t[i]) for i in range(16)]
    tt_mb = []
    ts_pj = []
    v_kv_cc = None
    _mb_bufs = [wbufa[:, 0:2048], wbufa[:, 2048:4096],
                wbufb[:, 0:2048], wbufb[:, 2048:4096]]
    for bi, (kind, i, src) in enumerate(mlist):
        buf = _mb_bufs[bi % 4]
        w = [("tt", tt_mb[bi - 4])] if bi >= 4 else [("tv", tv_p1)]
        vw = P.dma("gpsimd", buf, src, waits=w, sem="dg")
        psum = PSY if bi % 2 == 0 else PS0
        wmm = [("dg", vw)]
        if bi >= 2:
            wmm.append(("ts", ts_pj[bi - 2]))
        if bi == 0:
            wmm.append(("ts", ts_p1))
        def mk_mm(e, buf=buf, psum=psum):
            last = None
            for cch in range(16):
                for nh2 in range(2):
                    last = e.matmul(
                        psum[:, nh2 * 512:(nh2 + 1) * 512],
                        buf[:, cch * 128:(cch + 1) * 128],
                        xnT[:, cch * 1024 + nh2 * 512: cch * 1024 + (nh2 + 1) * 512],
                        start=(cch == 0), stop=(cch == 15))
            return last
        tt_mb.append(t_op(mk_mm, waits=wmm, inc=True))
        if kind == "k":
            dst = A2[:, i * 1024:(i + 1) * 1024]
            fn = (lambda e, dst=dst, psum=psum: e.copy(dst, psum[:, 0:1024]))
        elif kind == "v":
            dst = A2[:, 4096 + i * 1024:4096 + (i + 1) * 1024]
            fn = (lambda e, dst=dst, psum=psum: e.copy(dst, psum[:, 0:1024]))
        else:
            dst = A1[:, i * 1024:(i + 1) * 1024]
            fn = (lambda e, dst=dst, psum=psum: e.activation(
                dst, psum[:, 0:1024], AF.Identity, scale=QSCALE))
        ts_pj.append(s_op(fn, waits=[("tt", tt_mb[-1])], inc=True))
        if bi == 7:
            ts_kvs = ts_pj[-1]
            for m in range(4):
                P.dma("gpsimd", kv_in[m * 128:(m + 1) * 128, :],
                      A2[:, m * 1024:(m + 1) * 1024],
                      waits=([("ts", ts_kvs)] if m == 0 else ()), sem="dg")
            for m in range(4):
                P.dma("gpsimd", kv_in[512 + m * 128:512 + (m + 1) * 128, :],
                      A2[:, 4096 + m * 1024:4096 + (m + 1) * 1024], sem="dg")
            v_kv_cc = coll("AllGather", ALU.bypass,
                           [[0, 1], [2, 3], [4, 5], [6, 7]],
                           [kv_in], [kv_out], C_["dg"], "c_kv")

    for g in range(16):
        o, idx = OWNER[g]
        for kh in range(4):
            P.dma("gpsimd",
                  A2[:, kh * 2048 + g * 128: kh * 2048 + (g + 1) * 128],
                  kv_out[o * 1024 + kh * 128: o * 1024 + (kh + 1) * 128,
                         idx * 128:(idx + 1) * 128],
                  waits=([("c_kv", 1)] if (g == 0 and kh == 0) else ()), sem="dg")
    ts_vt = {}
    for g in range(16):
        o, idx = OWNER[g]
        wv = [("c_kv", 1)] if g == 0 else [("tt", tt_vtr[g - 1])]
        for vh in range(4):
            v_v = P.dma("gpsimd", vtmp[:, vh * 128:(vh + 1) * 128],
                        kv_out[o * 1024 + 512 + vh * 128: o * 1024 + 512 + (vh + 1) * 128,
                               idx * 128:(idx + 1) * 128],
                        waits=(wv if vh == 0 else ()), sem="dg")
        wt = [("dg", v_v)]
        wt.append(("ts", ts_xlt if g == 0 else ts_vt[g - 1]))
        def mk_vt(e):
            last = None
            for vh in range(4):
                last = e.transpose(PSPT[:, vh * 128:(vh + 1) * 128],
                                   vtmp[:, vh * 128:(vh + 1) * 128], eye_sb[:, 0:128])
            return last
        if g == 0:
            tt_vtr = {}
        tt_vtr[g] = t_op(mk_vt, waits=wt, inc=True)
        ts_vt[g] = s_op(lambda e, g=g: e.copy(
            A2[:, 8192 + g * 512:8192 + (g + 1) * 512], PSPT[:, 0:512]),
            waits=[("tt", tt_vtr[g])], inc=True)
    ts_kvdone = ts_vt[15]

    if debug:
        P.dma("gpsimd", dbg["dbg_xnT"], xnT[:, :], waits=[("ts", ts_p1)], sem="dg")
        P.dma("gpsimd", dbg["dbg_QT"], A1[:, :], waits=[("ts", ts_pj[-1])], sem="dg")
        P.dma("gpsimd", dbg["dbg_KV"], A2[:, :], waits=[("ts", ts_kvdone)], sem="dg")

    if upto <= 2:
        return nc, P, dbg
    KT = A2[:, 0:8192]
    Vt = A2[:, 8192:16384]

    it_S, it_PT, it_AV = {}, {}, {}
    it_mask, it_exp, it_scale, it_evac = {}, {}, {}, {}
    ts_yev = {}
    mk_dma = {}
    pt_guard = []
    th_ps_guard = []

    def plan_mask_dma(pos, w):
        slot = mk0 if pos % 2 == 0 else mk1
        if pos < 4:
            mk_dma[pos] = P.dma("sync", slot[:, 0:256],
                                masks[:, pos * 256:(pos + 1) * 256], waits=w)
        else:
            j = pos - 4
            P.dma("sync", slot[:, 0:512],
                  masks[:, 1024 + j * 512:1024 + (j + 1) * 512], waits=w)
            mk_dma[pos] = P.dma("sync", slot[:, 512:516],
                                masks[:, 3072 + j * 4:3072 + (j + 1) * 4])
    plan_mask_dma(0, [])
    plan_mask_dma(1, [])

    th_stream_guard = []

    def stream_chain(nch_, dram_grp_fn, ncols, lhsT_fn, bias_last_fn, first_waits,
                     gsz=4):
        ngr = (nch_ + gsz - 1) // gsz
        tts = []
        for g in range(ngr):
            buf = th_a if g % 2 == 0 else th_b
            n_in = min(gsz, nch_ - g * gsz)
            wd = [("tt", tts[g - 2])] if g >= 2 else list(th_stream_guard)
            vd = P.dma("gpsimd", buf.rearrange("p (k c) -> p k c", c=ncols)[:, 0:n_in, :],
                       dram_grp_fn(g * gsz, n_in), waits=wd, sem="dg")
            w = [("dg", vd)]
            if g == 0:
                w.extend(first_waits)
                w.extend(th_ps_guard)
            def mm(e, g=g, buf=buf, ncols=ncols, n_in=n_in):
                last = None
                for i in range(n_in):
                    cch = g * gsz + i
                    last = e.matmul(PST[0:4, 0:ncols], lhsT_fn(cch),
                                    buf[:, i * ncols:(i + 1) * ncols],
                                    start=(cch == 0), stop=False)
                    if cch == nch_ - 1:
                        last = e.matmul(PST[0:4, 0:ncols], ones_sb[32:33, 0:4],
                                        bias_last_fn(), start=False, stop=True)
                return last
            tts.append(t_op(mm, waits=w, inc=True))
        th_stream_guard[:] = [("tt", tts[-1])]
        return tts[-1]

    def emit_agg():
        tt_l = stream_chain(
            16, lambda c0, n: agg_bfd[c0:c0 + n].rearrange("g p c -> p g c"), AGN,
            lambda c: xlT[:, c * 4:(c + 1) * 4], lambda: aggb_row,
            [("ts", ts_kvdone), ("tt", tt_xlt)])
        tse = s_opd(lambda e: e.copy(tmp4[:, 0:AGN], PST[0:4, 0:AGN]),
                    waits=[("tt", tt_l)])
        th_ps_guard[:] = [("ts", tse)]
        P.dma("gpsimd", agg_ci, tmp4[:, 0:AGN], waits=[("ts", tse)], sem="dg")
        coll("AllGather", ALU.bypass, [list(range(8))], [agg_ci], [agg_co],
             C_["dg"], "c_agg")
        for r in range(8):
            P.dma("gpsimd", thought[:, r * AGN:(r + 1) * AGN],
                  agg_co[4 * r:4 * r + 4, :],
                  waits=([("c_agg", 1)] if r == 0 else ()), sem="dg")
        th_state["agg_done"] = C_["dg"]
        return C_["dg"]

    def small_transposes(src_bf, nch, dst, dst_cols, wait_tv, extra=()):
        w = [("tv", wait_tv)] + list(pt_guard) + list(extra)
        def mk(e, nch=nch, src_bf=src_bf):
            last = None
            for cch in range(nch):
                last = e.transpose(PSPT[0:128, 1536 + cch * 4:1536 + (cch + 1) * 4],
                                   src_bf[:, cch * 128:(cch + 1) * 128],
                                   eye_sb[0:4, 0:4])
            return last
        tt = t_op(mk, waits=w, inc=True)
        ts = s_opd(lambda e, dst=dst, dst_cols=dst_cols: e.copy(
            dst[:, 0:dst_cols], PSPT[:, 1536:1536 + dst_cols]),
            waits=[("tt", tt)])
        pt_guard[:] = [("ts", ts)]
        return ts

    th_state = {}

    def step_part1(step):
        st = th_state[step] = {}
        wf = []
        if step == 1:
            wf.append(("dg", th_state["agg_done"]))
        else:
            wf.append(("tv", th_state[step - 1]["tv_add"]))
        ln_rows(thought, tn_bf, 4, EPS_FFN, waits_first=wf)
        v_op(lambda e: e.tensor_mul(tn_bf, tn_bf, png4),
             waits=([("dg", v_png)] if step == 1 else ()))
        tv_tn = v_op(lambda e: e.tensor_add(tn_bf, tn_bf, pnb4), inc=True)
        ts_t = small_transposes(tn_bf, 16, tnT, 64, tv_tn)
        for osub in range(2):
            tt_l = stream_chain(
                16, lambda c0, n, osub=osub: d1_bfd[c0:c0 + n, :, osub * 512:(osub + 1) * 512]
                .rearrange("g p c -> p g c"),
                512, lambda c: tnT[:, c * 4:(c + 1) * 4],
                lambda osub=osub: d1b_row[:, osub * 512:(osub + 1) * 512],
                [("ts", ts_t)])
            ts_h = s_opd(lambda e, osub=osub: e.copy(
                tmp4[:, osub * 512:(osub + 1) * 512], PST[0:4, 0:512]),
                waits=[("tt", tt_l)])
            th_ps_guard[:] = [("ts", ts_h)]
        st["ts_g1"] = s_opd(lambda e: (e.drain(), e.activation(
            g1_bf, tmp4[:, 0:1024], AF.Gelu))[1])

    def step_part2(step):
        st = th_state[step]
        ts_gT = small_transposes(g1_bf, 8, g1T, 32, 0, extra=[("ts", st["ts_g1"])])
        ts_d = None
        for oc in range(4):
            tt_l = stream_chain(
                8, lambda c0, n, oc=oc: d2_bfd[c0:c0 + n, :, oc * 512:(oc + 1) * 512]
                .rearrange("g p c -> p g c"),
                512, lambda c: g1T[:, c * 4:(c + 1) * 4],
                lambda oc=oc: d2b8_row[:, oc * 512:(oc + 1) * 512],
                [("ts", ts_gT)])
            ts_d = s_opd(lambda e, oc=oc: e.copy(
                tmp4[:, oc * 512:(oc + 1) * 512], PST[0:4, 0:512]),
                waits=[("tt", tt_l)])
            th_ps_guard[:] = [("ts", ts_d)]
        P.dma("gpsimd", rs_ci, tmp4, waits=[("ts", ts_d)], sem="dg")
        st["c_n"] = coll("AllReduce", ALU.add, [list(range(8))], [rs_ci], [rs_co],
                         C_["dg"], "c_rs")

    def step_part3(step):
        st = th_state[step]
        v_back = P.dma("gpsimd", tmp4, rs_co, waits=[("c_rs", st["c_n"])], sem="dg")
        st["tv_add"] = v_op(lambda e: e.tensor_add(thought, thought, tmp4),
                            waits=[("dg", v_back)], inc=True)
        if debug:
            P.dma("gpsimd", dbg["dbg_steps"][:, step * 2048:(step + 1) * 2048],
                  thought, waits=[("tv", st["tv_add"])], sem="dg")

    def emit_bc_a():
        tv_c = v_op(lambda e: e.tensor_copy(tn_bf, thought),
                    waits=[("tv", th_state[4]["tv_add"])], inc=True)
        ts_t = small_transposes(tn_bf, 16, tnT, 64, tv_c)
        tt_l = stream_chain(
            16, lambda c0, n: bc_bfd[c0:c0 + n].rearrange("g p c -> p g c"), AGN,
            lambda c: tnT[:, c * 4:(c + 1) * 4], lambda: bcb_row, [("ts", ts_t)])
        tse = s_op(lambda e: e.copy(tmp4[:, 0:AGN], PST[0:4, 0:AGN]),
                   waits=[("tt", tt_l)], inc=True)
        th_ps_guard[:] = [("ts", tse)]
        P.dma("gpsimd", bc_ci, tmp4[:, 0:AGN], waits=[("ts", tse)], sem="dg")
        coll("AllGather", ALU.bypass, [list(range(8))], [bc_ci], [bc_co],
             C_["dg"], "c_bc")

    def emit_bc_b():
        for r in range(8):
            P.dma("gpsimd", thought[:, r * AGN:(r + 1) * AGN],
                  bc_co[4 * r:4 * r + 4, :],
                  waits=([("c_bc", 1)] if r == 0 else ()), sem="dg")
        return C_["dg"]

    v_agg_done = emit_agg()
    if debug:
        P.dma("gpsimd", dbg["dbg_steps"][:, 0:0 + 2048], thought,
              waits=[("dg", v_agg_done)], sem="dg")
        P.dma("gpsimd", dbg["dbg_xl"], xlT[:, 0:128],
              waits=[("ts", ts_xlt)], sem="dg")
    SCHED = {
        (1, 4): lambda: step_part1(1), (1, 12): lambda: step_part2(1),
        (2, 2): lambda: step_part3(1),
        (2, 6): lambda: step_part1(2), (2, 14): lambda: step_part2(2),
        (3, 2): lambda: step_part3(2),
        (3, 6): lambda: step_part1(3), (3, 14): lambda: step_part2(3),
        (4, 2): lambda: step_part3(3),
        (4, 6): lambda: step_part1(4), (4, 14): lambda: step_part2(4),
        (5, 2): lambda: step_part3(4),
        (5, 6): lambda: emit_bc_a(),
        (6, 2): lambda: emit_bc_b(),
    }
    v_bc_done = None

    for pos in range(NPOS):
        wc, chunks, mops, nch, ks, sink = s_layout(pos)
        slot = mk0 if pos % 2 == 0 else mk1

        def plan_S(it, h):
            kh = h // 4
            wS = [("ts", it_exp[it - 1])] if it > 0 else [("ts", ts_kvdone)]
            if h == 0:
                wS.append(("dq", mk_dma[pos]))
            def mk_S(e, pos=pos, h=h, kh=kh, chunks=chunks, sink=sink):
                last = None
                for (c0, n, kc0) in chunks:
                    last = e.matmul(PS0[:, c0:c0 + n],
                                    A1[:, h * 1024 + pos * 128: h * 1024 + (pos + 1) * 128],
                                    KT[:, kh * 2048 + kc0: kh * 2048 + kc0 + n],
                                    start=True, stop=True)
                if sink:
                    last = e.matmul(PS0[:, 1280:1284],
                                    A1[:, h * 1024 + pos * 128: h * 1024 + (pos + 1) * 128],
                                    KT[:, kh * 2048: kh * 2048 + 4],
                                    start=True, stop=True)
                return last
            it_S[it] = t_op(mk_S, waits=wS, inc=True)

        def plan_masks(it):
            def mk_M(e, mops=mops, slot=slot, sink=sink):
                last = None
                for (soff, w_, coff) in mops:
                    last = e.tensor_add(PS0[:, coff:coff + w_], PS0[:, coff:coff + w_],
                                        slot[:, soff:soff + w_])
                if sink:
                    last = e.tensor_add(PS0[:, 1280:1284], PS0[:, 1280:1284],
                                        slot[:, 512:516])
                return last
            it_mask[it] = v_op(mk_M, waits=[("tt", it_S[it])], inc=True)

        def plan_exp_scale(it):
            wE = [("tv", it_mask[it])]
            if it >= 1:
                wE.append(("tt", it_PT[it - 1]))
            def mk_E(e, wc=wc):
                return e.activation(P_sb[:, 0:wc], PS0[:, 0:wc], AF.Exp,
                                    accum_out=rowsum[:, 0:1])
            it_exp[it] = s_op(mk_E, waits=wE, inc=True)
            def mk_R(e, wc=wc):
                e.reciprocal(recip[:, 0:1], rowsum[:, 0:1])
                e.drain()
                return e.tensor_scalar_mul(P_sb[:, 0:wc], P_sb[:, 0:wc],
                                           recip[:, 0:1])
            it_scale[it] = v_op(mk_R, waits=[("ts", it_exp[it])], inc=True)

        def plan_ptav(it, h):
            kh = h // 4
            wPT = [("tv", it_scale[it])] + list(pt_guard)
            pt_guard[:] = []
            if it >= 1:
                wPT.append(("tv", it_evac[it - 1]))
            def mk_PT(e, nch=nch, sink=sink):
                last = None
                for cch in range(nch):
                    last = e.transpose(PSPT[:, cch * 128:(cch + 1) * 128],
                                       P_sb[:, cch * 128:(cch + 1) * 128],
                                       eye_sb[:, 0:128])
                if sink:
                    last = e.transpose(PSPT[0:4, 1408:1536],
                                       P_sb[:, 1280:1284], eye_sb[:, 0:128])
                return last
            it_PT[it] = t_op(mk_PT, waits=wPT, inc=True)
            ptb = PT0 if it % 2 == 0 else PT1
            wEV = [("tt", it_PT[it])]
            if it >= 2:
                wEV.append(("tt", it_AV[it - 2]))
            def mk_EV(e, ptb=ptb, nch=nch, sink=sink):
                last = e.tensor_copy(ptb[:, 0:nch * 128], PSPT[:, 0:nch * 128])
                if sink:
                    last = e.tensor_copy(ptb[0:4, 1408:1536], PSPT[0:4, 1408:1536])
                return last
            it_evac[it] = v_op(mk_EV, waits=wEV, inc=True)
            yslot = h % 8
            wAV = [("tv", it_evac[it])]
            if it >= 8:
                wAV.append(("ts", ts_yev[(it - 8) // 4]))
            def mk_AV(e, ptb=ptb, yslot=yslot, kh=kh, nch=nch, sink=sink, ks=ks):
                last = None
                for cch in range(nch):
                    g = ks + cch
                    last = e.matmul(PSY[:, yslot * 128:(yslot + 1) * 128],
                                    Vt[:, g * 512 + kh * 128: g * 512 + (kh + 1) * 128],
                                    ptb[:, cch * 128:(cch + 1) * 128],
                                    start=(cch == 0), stop=(cch == nch - 1 and not sink))
                if sink:
                    last = e.matmul(PSY[:, yslot * 128:(yslot + 1) * 128],
                                    Vt[0:4, kh * 128:(kh + 1) * 128],
                                    ptb[0:4, 1408:1536],
                                    start=False, stop=True)
                return last
            it_AV[it] = t_op(mk_AV, waits=wAV, inc=True)
            if h % 4 == 3:
                def mk_Y(e, pos=pos, h=h):
                    g4 = (h // 4) % 2
                    return e.copy(
                        YT[:, pos * 2048 + (h - 3) * 128: pos * 2048 + (h + 1) * 128],
                        PSY[:, g4 * 512:(g4 + 1) * 512])
                ts_yev[it // 4] = s_op(mk_Y, waits=[("tt", it_AV[it])], inc=True)

        for h in range(NH):
            it = pos * NH + h
            plan_S(it, h)
            plan_masks(it)
            if h >= 1:
                plan_ptav(it - 1, h - 1)
            plan_exp_scale(it)
            if (pos, h) in SCHED:
                r_ = SCHED[(pos, h)]()
                if (pos, h) == (6, 2):
                    v_bc_done = r_
        plan_ptav(pos * NH + NH - 1, NH - 1)
        if pos + 2 < NPOS:
            plan_mask_dma(pos + 2, [("tv", it_mask[pos * NH + NH - 1])])
    ts_attn = C_["ts"]
    tt_attn = C_["tt"]

    if debug:
        P.dma("gpsimd", dbg["dbg_yT"], YT[:, :],
              waits=[("ts", ts_yev[NPOS * NH // 4 - 1])], sem="dg")

    tv_bcc = v_op(lambda e: e.tensor_copy(tn_bf, thought),
                  waits=[("dg", v_bc_done)], inc=True)
    ts_prev = ts_attn
    bcast_jobs = [(tn_bf, bc_bcast, 0, "b1"), (tn_bf, bc_bcast, 1, "b1"),
                  (outg_row, g_bcast, 0, "p64"), (outg_row, g_bcast, 1, "p64"),
                  (outb_row, b_bcast, 0, "p0"), (outb_row, b_bcast, 1, "p0")]
    for srcrow, dstt, half, mode in bcast_jobs:
        w = [("ts", ts_prev)]
        if mode == "b1":
            w.append(("tv", tv_bcc))
        def mm(e, srcrow=srcrow, half=half, mode=mode):
            if mode == "b1":
                e.matmul(PS0[:, 0:512], b1_sb,
                         srcrow[:, half * 1024:half * 1024 + 512],
                         start=True, stop=True)
                return e.matmul(PS0[:, 512:1024], b1_sb,
                                srcrow[:, half * 1024 + 512:half * 1024 + 1024],
                                start=True, stop=True)
            lhs = ones_sb[64:65, 0:128] if mode == "p64" else ones_sb[0:1, 0:128]
            e.matmul(PS0[:, 0:512], lhs,
                     srcrow[:, half * 1024:half * 1024 + 512],
                     start=True, stop=True)
            return e.matmul(PS0[:, 512:1024], lhs,
                            srcrow[:, half * 1024 + 512:half * 1024 + 1024],
                            start=True, stop=True)
        tt_g = t_op(mm, waits=w, inc=True)
        ts_prev = s_op(lambda e, dstt=dstt, half=half: e.copy(
            dstt[:, half * 1024:(half + 1) * 1024], PS0[:, 0:1024]),
            waits=[("tt", tt_g)], inc=True)
    ts_bcast = ts_prev

    if debug:
        P.dma("gpsimd", dbg["dbg_bc"], bc_bcast, waits=[("ts", ts_bcast)], sem="dg")
        P.dma("gpsimd", dbg["dbg_th"], thought, waits=[("ts", ts_bcast)], sem="dg")

    if upto <= 3:
        return nc, P, dbg
    tt_wo = []
    ts_sv = []
    for j in range(8):
        buf = wbufa if j % 2 == 0 else wbufb
        w = [("tt", tt_wo[j - 2])] if j >= 2 else [("tt", tt_attn), ("ts", ts_attn)]
        vw = P.dma("gpsimd", buf, wo_t[j], waits=w, sem="dg")
        for pos in range(NPOS):
            jp = j * NPOS + pos
            bank = jp % 2
            wmm = [("dg", vw)] if pos == 0 else []
            if jp >= 2:
                wmm.append(("ts", ts_sv[jp - 2]))
            if jp == 0:
                wmm.append(("ts", ts_bcast))
            def mk_wo(e, j=j, pos=pos, buf=buf, bank=bank):
                last = None
                for hh in range(16):
                    last = e.matmul(
                        PSY[:, bank * 512:bank * 512 + 256],
                        YT[:, pos * 2048 + hh * 128: pos * 2048 + (hh + 1) * 128],
                        buf[:, hh * 256:(hh + 1) * 256],
                        start=(hh == 0), stop=(hh == 15))
                return last
            tt_l = t_op(mk_wo, waits=wmm, inc=True)
            ts_sv.append(s_op(lambda e, j=j, pos=pos, bank=bank: e.copy(
                A1[:, pos * 2048 + j * 256: pos * 2048 + (j + 1) * 256],
                PSY[:, bank * 512:bank * 512 + 256]),
                waits=[("tt", tt_l)], inc=True))
        tt_wo.append(tt_l)
    ts_wo_done = ts_sv[-1]

    if debug:
        P.dma("gpsimd", dbg["dbg_s"], A1[:, :], waits=[("ts", ts_wo_done)], sem="dg")

    if upto <= 4:
        return nc, P, dbg
    tt_gw = []
    ts_gv = []
    tv_u = []
    tg_u = []
    for j in range(8):
        buf = wbufa if j % 2 == 0 else wbufb
        w = [("tt", tt_gw[j - 2])] if j >= 2 else [("tt", tt_wo[-1]), ("ts", ts_wo_done)]
        vw = P.dma("gpsimd", buf, gw_t[j], waits=w, sem="dg")
        for pos in range(NPOS):
            jp = j * NPOS + pos
            bank = jp % 2
            wmm = [("dg", vw)] if pos == 0 else []
            if jp >= 2:
                wmm.append(("ts", ts_gv[jp - 2]))
            def mk_g(e, j=j, pos=pos, buf=buf, bank=bank):
                last = None
                for cch in range(16):
                    last = e.matmul(
                        PSY[:, bank * 512:bank * 512 + 256],
                        xnT[:, cch * 1024 + pos * 128: cch * 1024 + (pos + 1) * 128],
                        buf[:, cch * 256:(cch + 1) * 256],
                        start=(cch == 0), stop=False)
                return e.matmul(PSY[:, bank * 512:bank * 512 + 256],
                                ones_sb[32:33, 0:128],
                                gateb_row[:, j * 256:(j + 1) * 256],
                                start=False, stop=True)
            tt_l = t_op(mk_g, waits=wmm, inc=True)
            slc = (jp % 2) * 1024 + ((jp // 2) % 2) * 256
            wtr = []
            if jp >= 2:
                wtr.append(("tv", tv_u[jp - 2]))
            if jp == 0:
                wtr.extend(pt_guard)
                pt_guard[:] = []
                wtr.append(("tv", it_evac[NPOS * NH - 1]))
            def mk_xtr(e, j=j, pos=pos, slc=slc):
                e.transpose(PSPT[:, slc:slc + 128],
                            xnT[:, (2 * j) * 1024 + pos * 128:(2 * j) * 1024 + (pos + 1) * 128],
                            eye_sb[:, 0:128])
                return e.transpose(PSPT[:, slc + 128:slc + 256],
                                   xnT[:, (2 * j + 1) * 1024 + pos * 128:(2 * j + 1) * 1024 + (pos + 1) * 128],
                                   eye_sb[:, 0:128])
            tt_x = t_op(mk_xtr, waits=wtr, inc=True)
            gb = gtmpa if jp % 2 == 0 else gtmpb
            wsg = [("tt", tt_l)]
            if jp >= 2:
                wsg.append(("tg", tg_u[jp - 2]))
            ts_gv.append(s_op(lambda e, gb=gb, bank=bank: e.activation(
                gb[:, 0:256], PSY[:, bank * 512:bank * 512 + 256], AF.Sigmoid),
                waits=wsg, inc=True))
            ut = utmpa if jp % 2 == 0 else utmpb
            wut = [("ts", ts_gv[-1])]
            if jp == 0:
                wut.append(("ts", ts_bcast))
            if jp >= 2:
                wut.append(("tv", tv_u[jp - 2]))
            def mk_ut(e, gb=gb, ut=ut, j=j):
                return e.tensor_mul(ut[:, 0:256], gb[:, 0:256],
                                    bc_bcast[:, j * 256:(j + 1) * 256])
            P.op("gpsimd", mk_ut, waits=wut, incs=(("tg", 1),))
            tg_u.append(C_["tg"])
            piece = A2[:, pos * 2048 + j * 256: pos * 2048 + (j + 1) * 256]
            tv_u.append(v_op(lambda e, ut=ut, slc=slc, piece=piece: e.tensor_add(
                piece, ut[:, 0:256], PSPT[:, slc:slc + 256]),
                waits=[("tg", tg_u[-1]), ("tt", tt_x)], inc=True))
            ts_p1_ = s_op(lambda e, piece=piece: e.activation(
                sqb.bitcast(BF16)[:, 0:256], piece, AF.Identity,
                accum_out=pacc1[:, 0:1]), waits=[("tv", tv_u[-1])])
            ts_p2_ = s_op(lambda e, piece=piece: e.activation(
                sqb.bitcast(BF16)[:, 0:256], piece, AF.Square,
                accum_out=pacc2[:, 0:1]), inc=True)
            if j == 0:
                v_op(lambda e, pos=pos: e.tensor_copy(
                    usum[:, pos:pos + 1], pacc1[:, 0:1]),
                    waits=[("ts", ts_p2_)])
                v_op(lambda e, pos=pos: e.tensor_copy(
                    usq[:, pos:pos + 1], pacc2[:, 0:1]), inc=True)
            else:
                v_op(lambda e, pos=pos: e.tensor_add(
                    usum[:, pos:pos + 1], usum[:, pos:pos + 1], pacc1[:, 0:1]),
                    waits=[("ts", ts_p2_)])
                v_op(lambda e, pos=pos: e.tensor_add(
                    usq[:, pos:pos + 1], usq[:, pos:pos + 1], pacc2[:, 0:1]),
                    inc=True)
            tv_u[-1] = C_["tv"]
        tt_gw.append(tt_l)
    tv_u_done = tv_u[-1]

    if debug:
        P.dma("gpsimd", dbg["dbg_u"], A2[:, :], waits=[("tv", tv_u_done)], sem="dg")
        pass

    if upto <= 5:
        return nc, P, dbg
    for pos in range(NPOS):
        w0 = [("tv", tv_u_done)] if pos == 0 else ()
        v_op(lambda e, pos=pos: e.tensor_scalar(
            urs[:, pos:pos + 1], usq[:, pos:pos + 1], 1.0 / 2048.0, EPS_FFN,
            ALU.mult, ALU.add), waits=w0)
        v_op(lambda e, pos=pos: e.tensor_scalar_mul(
            une[:, pos:pos + 1], usum[:, pos:pos + 1], -1.0 / 2048.0))
        v_op(lambda e: e.drain())
        v_op(lambda e, pos=pos: e.tensor_mul(
            bag[:, 0:1], une[:, pos:pos + 1], une[:, pos:pos + 1]))
        v_op(lambda e: e.drain())
        v_op(lambda e, pos=pos: e.tensor_sub(
            urs[:, pos:pos + 1], urs[:, pos:pos + 1], bag[:, 0:1]))
    tv_ag = v_op(lambda e: e.drain(), inc=True)
    if debug:
        P.dma("gpsimd", dbg["dbg_us"][:, 0:8], usum[:, 0:8],
              waits=[("tv", tv_ag)], sem="dg")
        P.dma("gpsimd", dbg["dbg_us"][:, 8:16], usq[:, 0:8], sem="dg")
        P.dma("gpsimd", dbg["dbg_us"][:, 16:24], une[:, 0:8], sem="dg")
        P.dma("gpsimd", dbg["dbg_us"][:, 24:32], urs[:, 0:8], sem="dg")
    s_op(lambda e: (e.activation(urs[:, 0:NPOS], urs[:, 0:NPOS], AF.Ln),
                    e.drain())[0], waits=[("tv", tv_ag)])
    ts_rstd = s_op(lambda e: e.activation(urs[:, 0:NPOS], urs[:, 0:NPOS],
                                          AF.Exp, scale=-0.5), inc=True)
    ts_silu = {}
    tv_fin = {}
    dq_out = {}
    for pos in range(NPOS):
        xr = xre0 if pos % 2 == 0 else xre1
        wx = [("tv", tv_u_done)]
        if pos >= 2:
            wx.append(("dq", dq_out[pos - 2]))
        v_x = P.dma("sync", xr, x_own[pos * 128:(pos + 1) * 128, :], waits=wx)
        wv = [("ts", ts_rstd)]
        if pos >= 1:
            wv.append(("ts", ts_silu[pos - 1]))
        def mk_n(e, pos=pos):
            e.drain()
            return e.tensor_scalar(vt_f[:, 0:2048],
                                   A2[:, pos * 2048:(pos + 1) * 2048],
                                   une[:, pos:pos + 1], urs[:, pos:pos + 1],
                                   ALU.add, ALU.mult)
        v_op(mk_n, waits=wv)
        v_op(lambda e: e.tensor_mul(vt_f[:, 0:2048], vt_f[:, 0:2048], g_bcast))
        tv_v = v_op(lambda e: e.tensor_add(vt_f[:, 0:2048], vt_f[:, 0:2048], b_bcast),
                    inc=True)
        ts_silu[pos] = s_opd(lambda e: e.activation(vt_b[:, 0:2048], vt_f[:, 0:2048],
                                                    AF.Silu),
                             waits=[("tv", tv_v)])
        def mk_w(e, pos=pos):
            return e.tensor_mul(vt_f[:, 0:2048], vt_b[:, 0:2048],
                                A1[:, pos * 2048:(pos + 1) * 2048])
        v_op(mk_w, waits=[("ts", ts_silu[pos])])
        def mk_o(e, xr=xr, v_x=v_x):
            e.tensor_add(xr, xr, vt_f[:, 0:2048])
            return e.drain()
        tv_fin[pos] = v_op(mk_o, waits=[("dq", v_x)], inc=True)
        dq_out[pos] = P.dma("sync", out_own[pos * 128:(pos + 1) * 128, :], xr,
                            waits=[("tv", tv_fin[pos])])
    fin = C_["dq"]
    P.op("sync", lambda e: e.nop(), waits=[("dq", fin)])
    P.op("gpsimd", lambda e: e.nop(), waits=[("dq", fin), ("dg", C_["dg"])])

    return nc, P, dbg


def build_full(debug=False, upto=6):
    nc, P, dbg = build_program(debug=debug, upto=upto)
    with nc.Block() as block:
        @block.sync
        def _(e):
            P.replay("sync", e)

        @block.gpsimd
        def _(e):
            P.replay("gpsimd", e)

        @block.scalar
        def _(e):
            P.replay("scalar", e)

        @block.vector
        def _(e):
            P.replay("vector", e)

        @block.tensor
        def _(e):
            P.replay("tensor", e)
    return nc, dbg


def kernel(**inputs):
    x = np.asarray(inputs["L0_attn_x"])
    args = dict(
        Wq=np.asarray(inputs["L0_attn_Wq"]), Wk=np.asarray(inputs["L0_attn_Wk"]),
        Wv=np.asarray(inputs["L0_attn_Wv"]), Wo=np.asarray(inputs["L0_attn_Wo"]),
        in_g=np.asarray(inputs["L0_ffn_in_gamma"]), in_b=np.asarray(inputs["L0_ffn_in_beta"]),
        agg_w=np.asarray(inputs["L0_ffn_agg_w"]), agg_b=np.asarray(inputs["L0_ffn_agg_b"]),
        pn_g=np.asarray(inputs["L0_ffn_pn_gamma"]), pn_b=np.asarray(inputs["L0_ffn_pn_beta"]),
        d1_w=np.asarray(inputs["L0_ffn_d1_w"]), d1_b=np.asarray(inputs["L0_ffn_d1_b"]),
        d2_w=np.asarray(inputs["L0_ffn_d2_w"]), d2_b=np.asarray(inputs["L0_ffn_d2_b"]),
        bc_w=np.asarray(inputs["L0_ffn_bc_w"]), bc_b=np.asarray(inputs["L0_ffn_bc_b"]),
        gate_w=np.asarray(inputs["L0_ffn_gate_w"]), gate_b=np.asarray(inputs["L0_ffn_gate_b"]),
        out_g=np.asarray(inputs["L0_ffn_out_gamma"]), out_b=np.asarray(inputs["L0_ffn_out_beta"]),
    )
    shared = {
        "x_last": np.ascontiguousarray(x[:, -1, :]),
        "wq_t": _mblocks(np.ascontiguousarray(args["Wq"].T), 16),
        "wk_t": _mblocks(np.ascontiguousarray(args["Wk"].T), 4),
        "wv_t": _mblocks(np.ascontiguousarray(args["Wv"].T), 4),
        "wo_t": _osubs(np.ascontiguousarray(args["Wo"].T), 8, 256),
        "gw_t": _osubs(np.ascontiguousarray(args["gate_w"]), 8, 256),
        "gate_b": np.ascontiguousarray(args["gate_b"].reshape(1, C)),
        "d2_b": np.ascontiguousarray(args["d2_b"].reshape(1, C)),
        "out_g": np.ascontiguousarray(args["out_g"].reshape(1, C)),
        "out_be": np.ascontiguousarray(args["out_b"].reshape(1, C)),
        "in_g": np.ascontiguousarray(args["in_g"].reshape(1, C)),
        "in_b": np.ascontiguousarray(args["in_b"].reshape(1, C)),
        "pn_g": np.ascontiguousarray(args["pn_g"].reshape(1, C)),
        "pn_b": np.ascontiguousarray(args["pn_b"].reshape(1, C)),
    }
    in_maps = [make_core_inputs(c, x, shared=shared, **args) for c in range(8)]

    nc, _dbg = build_full(debug=False)
    res = run_bass_kernel_spmd(nc, in_maps, core_ids=list(range(8)))

    out = np.empty((4, 2048, C), np.float32)
    for c in range(8):
        b, par = c // 2, c % 2
        blocks = POS_BLOCKS[par]
        oc = res.results[c]["out_own"]
        for pos, g in enumerate(blocks):
            out[b, g * 128:(g + 1) * 128, :] = oc[pos * 128:(pos + 1) * 128, :]
    return out


# version 08 (final)
# speedup vs baseline: 1.1369x; 1.1369x over previous
import numpy as np
import ml_dtypes
import concourse.bass as bass
import concourse.mybir as mybir
from concourse.bass_utils import run_bass_kernel_spmd

F32 = mybir.dt.float32
BF16 = mybir.dt.bfloat16
AF = mybir.ActivationFunctionType
ALU = mybir.AluOpType
BF16NP = ml_dtypes.bfloat16

A_FH = [7, 4, 3, 0]
A_SH = [8, 11, 12, 15]
B_FH = [6, 5, 2, 1]
B_SH = [9, 10, 13, 14]
POS_BLOCKS = {0: A_FH + A_SH, 1: B_FH + B_SH}
A_SET = set(A_FH + A_SH)
FH_W = [8, 6, 4, 2]
SH_KS = [0, 2, 4, 6]
SH_NCH = 10
NEG = -30000.0
QSCALE = float(1.0 / np.sqrt(128.0))
EPS_OUTER = 1e-5
EPS_FFN = 1e-6

NPOS, NH, NKV, HS, C, TOK = 8, 16, 4, 128, 2048, 1024
D1N, AGN = 1024, 256

OWNER = {}
for _g in range(16):
    _par = 0 if _g in A_SET else 1
    OWNER[_g] = (_par, POS_BLOCKS[_par].index(_g))

ENGS = ("sync", "gpsimd", "scalar", "vector", "tensor")


def _mask_pack(par):
    i_ = np.arange(128)[:, None]
    j_ = np.arange(128)[None, :]
    tri_diag = np.where(j_ <= i_, 0.0, NEG)
    tri_edge = np.where(j_ > i_, 0.0, NEG)
    full = np.full((128, 128), NEG)
    zero = np.zeros((128, 128))
    tiles = []
    for i in range(4):
        Bq = POS_BLOCKS[par][i]
        W = FH_W[i]
        for cch in (W - 2, W - 1):
            tiles.append(tri_diag if cch == Bq else (full if cch > Bq else zero))
    for i in range(4):
        Bq = POS_BLOCKS[par][4 + i]
        ks = SH_KS[i]
        cbs = (Bq - 8) - ks
        cd = Bq - ks
        tiles.append(tri_edge if cbs == 0 else full)
        tiles.append(tri_edge if cbs == 1 else zero)
        tiles.append(tri_diag if cd == 8 else zero)
        tiles.append(tri_diag if cd == 9 else full)
    sinks = []
    for i in range(4):
        Bq = POS_BLOCKS[par][4 + i]
        if Bq == 8:
            sm = np.where(np.arange(4)[None, :] <= i_, 0.0, NEG)
        else:
            sm = np.zeros((128, 4))
        sinks.append(sm)
    arr = np.concatenate(tiles + sinks, axis=1)
    return np.ascontiguousarray(arr.astype(BF16NP))


def _mblocks(w_t, nblk):
    a = w_t.reshape(16, 128, nblk, 128).transpose(2, 1, 0, 3).reshape(nblk, 128, 2048)
    return np.ascontiguousarray(a)


def _osubs(w, nsub, sub):
    a = w.reshape(16, 128, nsub, sub).transpose(2, 1, 0, 3).reshape(nsub, 128, 16 * sub)
    return np.ascontiguousarray(a)


def make_core_inputs(c, x, Wq, Wk, Wv, Wo, in_g, in_b, agg_w, agg_b, pn_g, pn_b,
                     d1_w, d1_b, d2_w, d2_b, bc_w, bc_b, gate_w, gate_b,
                     out_g, out_b, shared):
    b, par = c // 2, c % 2
    blocks = POS_BLOCKS[par]
    xb = x[b].reshape(16, 128, C)
    x_own = np.ascontiguousarray(xb[blocks].reshape(TOK, C))
    one_hot = np.zeros((4, 128), BF16NP)
    one_hot[b, :] = BF16NP(1.0)
    d = dict(shared)
    d.update({
        "x_own": x_own,
        "masks": _mask_pack(par),
        "b1sel": one_hot,
        "agg_sl": np.ascontiguousarray(agg_w[:, c * AGN:(c + 1) * AGN].reshape(16, 128, AGN)),
        "bc_sl": np.ascontiguousarray(bc_w[:, c * AGN:(c + 1) * AGN].reshape(16, 128, AGN)),
        "d1_sl": np.ascontiguousarray(d1_w[:, c * D1N:(c + 1) * D1N].reshape(16, 128, D1N)),
        "d2_sl": np.ascontiguousarray(d2_w[c * D1N:(c + 1) * D1N, :].reshape(8, 128, C)),
        "d1_b": np.ascontiguousarray(d1_b[c * D1N:(c + 1) * D1N].reshape(1, D1N)),
        "agg_b": np.ascontiguousarray(agg_b[c * AGN:(c + 1) * AGN].reshape(1, AGN)),
        "bc_b": np.ascontiguousarray(bc_b[c * AGN:(c + 1) * AGN].reshape(1, AGN)),
    })
    return d


class Plan:
    def __init__(self, nc):
        self.nc = nc
        self.q = {e: [] for e in ENGS}
        self.cnt = {}
        self.sem = {}

    def new_sem(self, name):
        self.sem[name] = self.nc.alloc_semaphore(name)
        self.cnt[name] = 0

    def op(self, eng, fn, waits=(), incs=()):
        resolved = []
        for sname, amt in incs:
            if amt is None:
                self.cnt[sname] += 1
            else:
                self.cnt[sname] += amt
            resolved.append((self.sem[sname], amt))
        wl = [(self.sem[s], v) for s, v in waits]
        self.q[eng].append((wl, fn, resolved))

    def dma(self, eng, out_ap, in_ap, waits=(), sem="dq"):
        def fn(e, o=out_ap, i=in_ap):
            return e.dma_start(out=o, in_=i)
        self.op(eng, fn, waits=waits, incs=((sem, 16),))
        return self.cnt[sem]

    def replay(self, eng, e):
        for wl, fn, incs in self.q[eng]:
            for h, v in wl:
                e.wait_ge(h, v)
            ins = fn(e)
            for h, amt in incs:
                assert ins is not None, f"{eng}: inc on op that returned None"
                ins = ins.then_inc(h) if amt is None else ins.then_inc(h, amt)


def s_layout(pos):
    if pos < 4:
        W = FH_W[pos]
        wc = W * 128
        chunks, c0 = [], 0
        while c0 < wc:
            n = min(512, wc - c0)
            chunks.append((c0, n, c0))
            c0 += n
        mops = [(0, 128, (W - 2) * 128), (128, 128, (W - 1) * 128)]
        return wc, chunks, mops, W, 0, False
    j = pos - 4
    ks = SH_KS[j]
    wc = SH_NCH * 128 + 4
    chunks = [(0, 512, ks * 128), (512, 512, ks * 128 + 512), (1024, 256, ks * 128 + 1024)]
    mops = [(0, 128, 0), (128, 128, 128), (256, 128, 8 * 128), (384, 128, 9 * 128)]
    return wc, chunks, mops, SH_NCH, ks, True


def build_program(debug=False, upto=6):
    nc = bass.Bass("TRN2", target_bir_lowering=False, debug=False)
    P = Plan(nc)
    for s in ("dq", "dg", "tv", "ts", "tt", "tg", "c_kv", "c_agg", "c_rs", "c_bc"):
        P.new_sem(s)
    S = P.sem
    C_ = P.cnt

    def inp(name, shape, dt=F32):
        return nc.dram_tensor(name, shape, dt, kind="ExternalInput").ap()

    x_own = inp("x_own", [TOK, C])
    x_last = inp("x_last", [4, C])
    wq_t = inp("wq_t", [16, 128, 2048])
    wk_t = inp("wk_t", [4, 128, 2048])
    wv_t = inp("wv_t", [4, 128, 2048])
    wo_t = inp("wo_t", [8, 128, 16 * 256])
    gw_t = inp("gw_t", [8, 128, 16 * 256])
    agg_sl = inp("agg_sl", [16, 128, AGN])
    bc_sl = inp("bc_sl", [16, 128, AGN])
    d1_sl = inp("d1_sl", [16, 128, D1N])
    d2_sl = inp("d2_sl", [8, 128, C])
    masks = inp("masks", [128, 3088], BF16)
    b1sel = inp("b1sel", [4, 128], BF16)
    gate_b = inp("gate_b", [1, C])
    d1_b = inp("d1_b", [1, D1N])
    agg_b = inp("agg_b", [1, AGN])
    bc_b = inp("bc_b", [1, AGN])
    d2_b = inp("d2_b", [1, C])
    out_g = inp("out_g", [1, C])
    out_be = inp("out_be", [1, C])
    in_g = inp("in_g", [1, C])
    in_b = inp("in_b", [1, C])
    pn_g = inp("pn_g", [1, C])
    pn_b = inp("pn_b", [1, C])

    out_own = nc.dram_tensor("out_own", [TOK, C], F32, kind="ExternalOutput").ap()

    dbg = {}
    if debug:
        for name, shape, dt in [
            ("dbg_xnT", [128, 16 * TOK], BF16), ("dbg_QT", [128, 16384], BF16),
            ("dbg_KV", [128, 16384], BF16), ("dbg_yT", [128, 16384], BF16),
            ("dbg_s", [128, 16384], BF16), ("dbg_th", [4, 2048], F32),
            ("dbg_steps", [4, 6 * 2048], F32), ("dbg_xl", [128, 128], BF16),
            ("dbg_tn", [4, 2048], BF16), ("dbg_g1", [4, 1024], BF16),
            ("dbg_h1", [4, 1024], F32), ("dbg_us", [128, 32], F32),
            ("dbg_part", [4, 2048], F32), ("dbg_red", [4, 2048], F32),
            ("dbg_bc", [128, 2048], F32), ("dbg_u", [128, 16384], BF16),
        ]:
            dbg[name] = nc.dram_tensor(name, shape, dt, kind="ExternalOutput").ap()

    kv_in = nc.dram_tensor("kv_in", [1024, TOK], BF16, kind="Internal").ap()
    kv_out = nc.dram_tensor("kv_out", [2048, TOK], BF16, kind="Internal").ap()
    d1_bfd = nc.dram_tensor("d1_bfd", [16, 128, D1N], BF16, kind="Internal").ap()
    d2_bfd = nc.dram_tensor("d2_bfd", [8, 128, C], BF16, kind="Internal").ap()
    agg_bfd = nc.dram_tensor("agg_bfd", [16, 128, AGN], BF16, kind="Internal").ap()
    bc_bfd = nc.dram_tensor("bc_bfd", [16, 128, AGN], BF16, kind="Internal").ap()
    agg_ci = nc.dram_tensor("agg_ci", [4, AGN], F32, kind="Internal").ap()
    agg_co = nc.dram_tensor("agg_co", [32, AGN], F32, kind="Internal").ap()
    rs_ci = nc.dram_tensor("rs_ci", [4, C], F32, kind="Internal").ap()
    rs_co = nc.dram_tensor("rs_co", [4, C], F32, kind="Internal").ap()
    bc_ci = nc.dram_tensor("bc_ci", [4, AGN], F32, kind="Internal").ap()
    bc_co = nc.dram_tensor("bc_co", [32, AGN], F32, kind="Internal").ap()

    eye = nc.inline_tensor(np.eye(128, dtype=BF16NP), name="eye128").ap()

    A = nc.alloc_sbuf_tensor
    xnT = A("xnT", [128, 16 * TOK], BF16).ap()
    A1 = A("A1", [128, 16384], BF16).ap()
    A2 = A("A2", [128, 16384], BF16).ap()
    YT = A("YT", [128, 16384], BF16).ap()
    STG = A("STG", [128, 8192], BF16).ap()
    TE = A("TE", [128, 4096], F32).ap()
    SM = A("SM", [128, 10560], F32).ap()

    stXa = STG[:, 0:4096].bitcast(F32)
    stXb = STG[:, 4096:8192].bitcast(F32)
    wbufa = STG[:, 0:4096]
    wbufb = STG[:, 4096:8192]
    th_a = STG[:, 0:2048]
    th_b = STG[:, 2048:4096]
    vt_f = STG[:, 0:4096].bitcast(F32)
    vt_b = STG[:, 4096:6144]

    thought = TE[0:4, 0:2048]
    tmp4 = TE[0:4, 2048:4096]
    bc_bcast = TE[:, 0:2048]
    g_bcast = TE[:, 2048:3072].bitcast(BF16)
    b_bcast = TE[:, 3072:4096].bitcast(BF16)

    _off = [0]
    def carve(n):
        ap = SM[:, _off[0]:_off[0] + n]
        _off[0] += n
        return ap
    P_sb = carve(656).bitcast(BF16)
    PT0 = carve(768).bitcast(BF16)
    PT1 = carve(768).bitcast(BF16)
    mk0 = carve(264).bitcast(BF16)
    mk1 = carve(264).bitcast(BF16)
    eye_sb = carve(64).bitcast(BF16)
    ones_sb = carve(64).bitcast(BF16)
    b1_sb = carve(64).bitcast(BF16)[0:4, :]
    rowsum = carve(8)
    recip = carve(8)
    bst = carve(32)
    bag = carve(8)
    nmu = carve(8)
    rsd = carve(8)
    une = carve(8)
    urs = carve(8)
    usum = carve(8)
    usq = carve(8)
    pacc1 = carve(8)
    pacc2 = carve(8)
    sqb = carve(256)
    thv = carve(1024)
    tn_bf = thv.bitcast(BF16)[0:4, :]
    d2b8_row = thv.bitcast(BF16)[32:33, :]
    png4 = carve(1024).bitcast(BF16)[0:4, :]
    pnb4 = carve(1024).bitcast(BF16)[0:4, :]
    g1_reg = carve(512)
    g1_bf = g1_reg.bitcast(BF16)[0:4, :]
    d1b_row = g1_reg.bitcast(BF16)[32:33, :]
    xlT = carve(64).bitcast(BF16)
    tnT = carve(64).bitcast(BF16)
    g1T = carve(32).bitcast(BF16)
    xnbuf = carve(1024).bitcast(BF16)
    vtmp = carve(256).bitcast(BF16)
    utmpa = carve(256)
    utmpb = carve(256)
    gt_reg = carve(128)
    gtmpa = gt_reg.bitcast(BF16)
    aggb_row = gt_reg.bitcast(BF16)[32:33, 0:256]
    gt_reg2 = carve(128)
    gtmpb = gt_reg2.bitcast(BF16)
    bcb_row = gt_reg2.bitcast(BF16)[32:33, 0:256]
    rows3 = carve(1024).bitcast(BF16)
    gateb_row = rows3[32:33, :]
    outg_row = rows3[64:65, :]
    outb_row = rows3[0:1, :]
    assert _off[0] <= 10560, _off[0]

    xre0 = xnT[:, 0:4096].bitcast(F32)
    xre1 = xnT[:, 4096:8192].bitcast(F32)

    PS0 = nc.alloc_psum_tensor("PS0", [128, 1536], F32).ap()
    PSPT = nc.alloc_psum_tensor("PSPT", [128, 2048], BF16).ap()
    PSY = nc.alloc_psum_tensor("PSY", [128, 1024], F32).ap()
    PST = nc.alloc_psum_tensor("PST", [128, 512], F32).ap()

    def t_op(fn, waits=(), inc=False):
        P.op("tensor", fn, waits=waits, incs=((("tt", 1),) if inc else ()))
        return C_["tt"]

    def v_op(fn, waits=(), inc=False):
        P.op("vector", fn, waits=waits, incs=((("tv", 1),) if inc else ()))
        return C_["tv"]

    def s_op(fn, waits=(), inc=False):
        P.op("scalar", fn, waits=waits, incs=((("ts", 1),) if inc else ()))
        return C_["ts"]

    def s_opd(fn, waits=(), inc=True):
        def wrap(e, fn=fn):
            fn(e)
            return e.drain()
        P.op("scalar", wrap, waits=waits, incs=((("ts", 1),) if inc else ()))
        return C_["ts"]

    def coll(kind, op, groups, ins, outs, wait_dg, sem):
        def fn(e, kind=kind, op=op, groups=groups, ins=ins, outs=outs):
            return e.collective_compute(kind, op, replica_groups=groups,
                                        ins=ins, outs=outs)
        P.op("gpsimd", fn, waits=[("dg", wait_dg)], incs=((sem, None),))
        return C_[sem]

    def ln_rows(src, dst, nrow, eps, junk=None, waits_first=()):
        if junk is None:
            junk = dst
        n = 2048.0
        ts1 = s_op(lambda e, src=src, junk=junk: e.activation(
            junk, src, AF.Identity, accum_out=bag[0:nrow, 0:1]),
            waits=waits_first, inc=True)
        tv1 = v_op(lambda e, nrow=nrow: e.tensor_scalar_mul(
            nmu[0:nrow, 0:1], bag[0:nrow, 0:1], -1.0 / n),
            waits=[("ts", ts1)], inc=True)
        ts2 = s_op(lambda e, src=src, junk=junk, nrow=nrow: e.activation(
            junk, src, AF.Square, bias=nmu[0:nrow, 0:1],
            accum_out=bag[0:nrow, 1:2]), waits=[("tv", tv1)], inc=True)
        tv2 = v_op(lambda e, nrow=nrow, eps=eps: e.tensor_scalar(
            rsd[0:nrow, 0:1], bag[0:nrow, 1:2], float(n * eps), 1.0 / n,
            ALU.add, ALU.mult), waits=[("ts", ts2)], inc=True)
        s_op(lambda e, nrow=nrow: (e.activation(
            rsd[0:nrow, 0:1], rsd[0:nrow, 0:1], AF.Ln), e.drain())[0],
            waits=[("tv", tv2)])
        ts_r = s_op(lambda e, nrow=nrow: e.activation(
            rsd[0:nrow, 0:1], rsd[0:nrow, 0:1], AF.Exp, scale=-0.5), inc=True)
        return v_op(lambda e, src=src, dst=dst, nrow=nrow: (e.drain(), e.tensor_scalar(
            dst, src, nmu[0:nrow, 0:1], rsd[0:nrow, 0:1], ALU.add, ALU.mult))[1],
            waits=[("ts", ts_r)], inc=True)

    P.dma("gpsimd", eye_sb[:, 0:128], eye, sem="dg")
    P.dma("gpsimd", b1_sb, b1sel, sem="dg")
    v_eye = C_["dg"]
    P.dma("gpsimd", d1_bfd, d1_sl, sem="dg")
    P.dma("gpsimd", d2_bfd, d2_sl, sem="dg")
    P.dma("gpsimd", agg_bfd, agg_sl, sem="dg")
    P.dma("gpsimd", bc_bfd, bc_sl, sem="dg")
    P.dma("gpsimd", gateb_row, gate_b, sem="dg")
    P.dma("gpsimd", outg_row, out_g, sem="dg")
    P.dma("gpsimd", outb_row, out_be, sem="dg")
    P.dma("gpsimd", d1b_row, d1_b, sem="dg")
    P.dma("gpsimd", aggb_row, agg_b, sem="dg")
    P.dma("gpsimd", bcb_row, bc_b, sem="dg")
    for r in range(4):
        P.dma("gpsimd", png4[r:r + 1, :], in_g, sem="dg")
        P.dma("gpsimd", pnb4[r:r + 1, :], in_b, sem="dg")
    v_ing = C_["dg"]
    v_d2b = P.dma("gpsimd", d2b8_row, d2_b, sem="dg")
    v_op(lambda e: e.tensor_scalar_mul(d2b8_row, d2b8_row, 0.125),
         waits=[("dg", v_d2b)])
    v_op(lambda e: e.memset(ones_sb[:, 0:128], 1.0), inc=True)
    v_xlast = P.dma("gpsimd", tmp4, x_last, sem="dg")

    tv_norm = {}
    ts_evac = {}
    tt_tr = {}
    for pos in range(NPOS):
        st = stXa if pos % 2 == 0 else stXb
        w = [("tv", tv_norm[pos - 2])] if pos >= 2 else []
        v_x = P.dma("sync", st, x_own[pos * 128:(pos + 1) * 128, :], waits=w)
        wv = [("dq", v_x)]
        if pos >= 1:
            wv.append(("tt", tt_tr[pos - 1]))
        ts1 = s_op(lambda e, st=st: e.activation(
            xnbuf[:, 0:2048], st, AF.Identity, accum_out=bag[:, 0:1]),
            waits=wv, inc=True)
        tv1 = v_op(lambda e: e.tensor_scalar_mul(nmu[:, 0:1], bag[:, 0:1],
                                                 -1.0 / 2048.0),
                   waits=[("ts", ts1)], inc=True)
        ts2 = s_op(lambda e, st=st: e.activation(
            xnbuf[:, 0:2048], st, AF.Square, bias=nmu[:, 0:1],
            accum_out=bag[:, 1:2]), waits=[("tv", tv1)], inc=True)
        tv2 = v_op(lambda e: e.tensor_scalar(
            rsd[:, 0:1], bag[:, 1:2], 2048.0 * EPS_OUTER, 1.0 / 2048.0,
            ALU.add, ALU.mult), waits=[("ts", ts2)], inc=True)
        s_op(lambda e: (e.activation(rsd[:, 0:1], rsd[:, 0:1], AF.Ln),
                        e.drain())[0], waits=[("tv", tv2)])
        ts_r = s_op(lambda e: e.activation(rsd[:, 0:1], rsd[:, 0:1], AF.Exp, scale=-0.5),
                    inc=True)
        tv_norm[pos] = v_op(lambda e, st=st: (e.drain(), e.tensor_scalar(
            xnbuf[:, 0:2048], st, nmu[:, 0:1], rsd[:, 0:1], ALU.add, ALU.mult))[1],
            waits=[("ts", ts_r)], inc=True)
        wt = [("tv", tv_norm[pos])]
        if pos == 0:
            wt.append(("dg", v_eye))
        else:
            wt.append(("ts", ts_evac[pos - 1]))
        def mk_tr(e):
            last = None
            for cch in range(16):
                last = e.transpose(PSPT[:, cch * 128:(cch + 1) * 128],
                                   xnbuf[:, cch * 128:(cch + 1) * 128],
                                   eye_sb[:, 0:128])
            return last
        tt_tr[pos] = t_op(mk_tr, waits=wt, inc=True)
        xnT3 = xnT.rearrange("p (c t) -> p c t", t=TOK)
        def mk_ev(e, pos=pos):
            last = None
            for half in range(2):
                src = PSPT[:, half * 1024:(half + 1) * 1024].rearrange(
                    "p (c q) -> p c q", q=128)
                dst = xnT3[:, half * 8:(half + 1) * 8, pos * 128:(pos + 1) * 128]
                last = e.copy(dst, src)
            return last
        ts_evac[pos] = s_op(mk_ev, waits=[("tt", tt_tr[pos])], inc=True)
    tv_p1 = C_["tv"]
    ts_p1 = C_["ts"]

    tv_l1 = ln_rows(tmp4, thought, 4, EPS_OUTER,
                    waits_first=[("dg", v_xlast)])
    ln_rows(thought, tn_bf, 4, EPS_FFN,
            waits_first=[("tv", tv_l1)])
    v_op(lambda e: e.tensor_mul(tn_bf, tn_bf, png4), waits=[("dg", v_ing)])
    tv_xln = v_op(lambda e: e.tensor_add(tn_bf, tn_bf, pnb4), inc=True)
    v_png = 0
    for r in range(4):
        P.dma("gpsimd", png4[r:r + 1, :], pn_g,
              waits=([("tv", tv_xln)] if r == 0 else ()), sem="dg")
        v_png = P.dma("gpsimd", pnb4[r:r + 1, :], pn_b, sem="dg")
    def mk_xlt(e):
        last = None
        for cch in range(16):
            last = e.transpose(PSPT[0:128, 1536 + cch * 4:1536 + (cch + 1) * 4],
                               tn_bf[:, cch * 128:(cch + 1) * 128], eye_sb[0:4, 0:4])
        return last
    tt_xlt = t_op(mk_xlt, waits=[("tv", tv_xln), ("ts", ts_evac[NPOS - 1])], inc=True)
    ts_xlt = s_opd(lambda e: e.copy(xlT[:, 0:64], PSPT[:, 1536:1600]),
                   waits=[("tt", tt_xlt)])
    if debug and upto <= 1:
        P.dma("gpsimd", dbg["dbg_xnT"], xnT[:, :], waits=[("ts", ts_p1)], sem="dg")
    if upto <= 1:
        return nc, P, dbg

    mlist = [("k", i, wk_t[i]) for i in range(4)] + \
            [("v", i, wv_t[i]) for i in range(4)] + \
            [("q", i, wq_t[i]) for i in range(16)]
    tt_mb = []
    ts_pj = []
    v_kv_cc = None
    _mb_bufs = [wbufa[:, 0:2048], wbufa[:, 2048:4096],
                wbufb[:, 0:2048], wbufb[:, 2048:4096]]
    for bi, (kind, i, src) in enumerate(mlist):
        buf = _mb_bufs[bi % 4]
        w = [("tt", tt_mb[bi - 4])] if bi >= 4 else [("tv", tv_p1)]
        vw = P.dma("gpsimd", buf, src, waits=w, sem="dg")
        psum = PSY if bi % 2 == 0 else PS0
        wmm = [("dg", vw)]
        if bi >= 2:
            wmm.append(("ts", ts_pj[bi - 2]))
        if bi == 0:
            wmm.append(("ts", ts_p1))
        def mk_mm(e, buf=buf, psum=psum):
            last = None
            for cch in range(16):
                for nh2 in range(2):
                    last = e.matmul(
                        psum[:, nh2 * 512:(nh2 + 1) * 512],
                        buf[:, cch * 128:(cch + 1) * 128],
                        xnT[:, cch * 1024 + nh2 * 512: cch * 1024 + (nh2 + 1) * 512],
                        start=(cch == 0), stop=(cch == 15))
            return last
        tt_mb.append(t_op(mk_mm, waits=wmm, inc=True))
        if kind == "k":
            dst = A2[:, i * 1024:(i + 1) * 1024]
            fn = (lambda e, dst=dst, psum=psum: e.copy(dst, psum[:, 0:1024]))
        elif kind == "v":
            dst = A2[:, 4096 + i * 1024:4096 + (i + 1) * 1024]
            fn = (lambda e, dst=dst, psum=psum: e.copy(dst, psum[:, 0:1024]))
        else:
            dst = A1[:, i * 1024:(i + 1) * 1024]
            fn = (lambda e, dst=dst, psum=psum: e.activation(
                dst, psum[:, 0:1024], AF.Identity, scale=QSCALE))
        ts_pj.append(s_op(fn, waits=[("tt", tt_mb[-1])], inc=True))
        if bi == 7:
            ts_kvs = ts_pj[-1]
            for m in range(4):
                P.dma("gpsimd", kv_in[m * 128:(m + 1) * 128, :],
                      A2[:, m * 1024:(m + 1) * 1024],
                      waits=([("ts", ts_kvs)] if m == 0 else ()), sem="dg")
            for m in range(4):
                P.dma("gpsimd", kv_in[512 + m * 128:512 + (m + 1) * 128, :],
                      A2[:, 4096 + m * 1024:4096 + (m + 1) * 1024], sem="dg")
            v_kv_cc = coll("AllGather", ALU.bypass,
                           [[0, 1], [2, 3], [4, 5], [6, 7]],
                           [kv_in], [kv_out], C_["dg"], "c_kv")

    for g in range(16):
        o, idx = OWNER[g]
        for kh in range(4):
            P.dma("gpsimd",
                  A2[:, kh * 2048 + g * 128: kh * 2048 + (g + 1) * 128],
                  kv_out[o * 1024 + kh * 128: o * 1024 + (kh + 1) * 128,
                         idx * 128:(idx + 1) * 128],
                  waits=([("c_kv", 1)] if (g == 0 and kh == 0) else ()), sem="dg")
    ts_vt = {}
    for g in range(16):
        o, idx = OWNER[g]
        wv = [("c_kv", 1)] if g == 0 else [("tt", tt_vtr[g - 1])]
        for vh in range(4):
            v_v = P.dma("gpsimd", vtmp[:, vh * 128:(vh + 1) * 128],
                        kv_out[o * 1024 + 512 + vh * 128: o * 1024 + 512 + (vh + 1) * 128,
                               idx * 128:(idx + 1) * 128],
                        waits=(wv if vh == 0 else ()), sem="dg")
        wt = [("dg", v_v)]
        wt.append(("ts", ts_xlt if g == 0 else ts_vt[g - 1]))
        def mk_vt(e):
            last = None
            for vh in range(4):
                last = e.transpose(PSPT[:, vh * 128:(vh + 1) * 128],
                                   vtmp[:, vh * 128:(vh + 1) * 128], eye_sb[:, 0:128])
            return last
        if g == 0:
            tt_vtr = {}
        tt_vtr[g] = t_op(mk_vt, waits=wt, inc=True)
        ts_vt[g] = s_op(lambda e, g=g: e.copy(
            A2[:, 8192 + g * 512:8192 + (g + 1) * 512], PSPT[:, 0:512]),
            waits=[("tt", tt_vtr[g])], inc=True)
    ts_kvdone = ts_vt[15]

    if debug:
        P.dma("gpsimd", dbg["dbg_xnT"], xnT[:, :], waits=[("ts", ts_p1)], sem="dg")
        P.dma("gpsimd", dbg["dbg_QT"], A1[:, :], waits=[("ts", ts_pj[-1])], sem="dg")
        P.dma("gpsimd", dbg["dbg_KV"], A2[:, :], waits=[("ts", ts_kvdone)], sem="dg")

    if upto <= 2:
        return nc, P, dbg
    KT = A2[:, 0:8192]
    Vt = A2[:, 8192:16384]

    it_S, it_PT, it_AV = {}, {}, {}
    it_mask, it_exp, it_scale, it_evac = {}, {}, {}, {}
    ts_yev = {}
    mk_dma = {}
    pt_guard = []
    th_ps_guard = []

    def plan_mask_dma(pos, w):
        slot = mk0 if pos % 2 == 0 else mk1
        if pos < 4:
            mk_dma[pos] = P.dma("sync", slot[:, 0:256],
                                masks[:, pos * 256:(pos + 1) * 256], waits=w)
        else:
            j = pos - 4
            P.dma("sync", slot[:, 0:512],
                  masks[:, 1024 + j * 512:1024 + (j + 1) * 512], waits=w)
            mk_dma[pos] = P.dma("sync", slot[:, 512:516],
                                masks[:, 3072 + j * 4:3072 + (j + 1) * 4])
    plan_mask_dma(0, [])
    plan_mask_dma(1, [])

    th_stream_guard = []

    def stream_chain(nch_, dram_grp_fn, ncols, lhsT_fn, bias_last_fn, first_waits,
                     gsz=4):
        ngr = (nch_ + gsz - 1) // gsz
        tts = []
        for g in range(ngr):
            buf = th_a if g % 2 == 0 else th_b
            n_in = min(gsz, nch_ - g * gsz)
            wd = [("tt", tts[g - 2])] if g >= 2 else list(th_stream_guard)
            vd = P.dma("gpsimd", buf.rearrange("p (k c) -> p k c", c=ncols)[:, 0:n_in, :],
                       dram_grp_fn(g * gsz, n_in), waits=wd, sem="dg")
            w = [("dg", vd)]
            if g == 0:
                w.extend(first_waits)
                w.extend(th_ps_guard)
            def mm(e, g=g, buf=buf, ncols=ncols, n_in=n_in):
                last = None
                for i in range(n_in):
                    cch = g * gsz + i
                    last = e.matmul(PST[0:4, 0:ncols], lhsT_fn(cch),
                                    buf[:, i * ncols:(i + 1) * ncols],
                                    start=(cch == 0), stop=False)
                    if cch == nch_ - 1:
                        last = e.matmul(PST[0:4, 0:ncols], ones_sb[32:33, 0:4],
                                        bias_last_fn(), start=False, stop=True)
                return last
            tts.append(t_op(mm, waits=w, inc=True))
        th_stream_guard[:] = [("tt", tts[-1])]
        return tts[-1]

    def emit_agg():
        tt_l = stream_chain(
            16, lambda c0, n: agg_bfd[c0:c0 + n].rearrange("g p c -> p g c"), AGN,
            lambda c: xlT[:, c * 4:(c + 1) * 4], lambda: aggb_row,
            [("ts", ts_kvdone), ("tt", tt_xlt)])
        tse = s_opd(lambda e: e.copy(tmp4[:, 0:AGN], PST[0:4, 0:AGN]),
                    waits=[("tt", tt_l)])
        th_ps_guard[:] = [("ts", tse)]
        P.dma("gpsimd", agg_ci, tmp4[:, 0:AGN], waits=[("ts", tse)], sem="dg")
        coll("AllGather", ALU.bypass, [list(range(8))], [agg_ci], [agg_co],
             C_["dg"], "c_agg")
        for r in range(8):
            P.dma("gpsimd", thought[:, r * AGN:(r + 1) * AGN],
                  agg_co[4 * r:4 * r + 4, :],
                  waits=([("c_agg", 1)] if r == 0 else ()), sem="dg")
        th_state["agg_done"] = C_["dg"]
        return C_["dg"]

    def small_transposes(src_bf, nch, dst, dst_cols, wait_tv, extra=()):
        w = [("tv", wait_tv)] + list(pt_guard) + list(extra)
        def mk(e, nch=nch, src_bf=src_bf):
            last = None
            for cch in range(nch):
                last = e.transpose(PSPT[0:128, 1536 + cch * 4:1536 + (cch + 1) * 4],
                                   src_bf[:, cch * 128:(cch + 1) * 128],
                                   eye_sb[0:4, 0:4])
            return last
        tt = t_op(mk, waits=w, inc=True)
        ts = s_opd(lambda e, dst=dst, dst_cols=dst_cols: e.copy(
            dst[:, 0:dst_cols], PSPT[:, 1536:1536 + dst_cols]),
            waits=[("tt", tt)])
        pt_guard[:] = [("ts", ts)]
        return ts

    th_state = {}

    def step_part1(step):
        st = th_state[step] = {}
        wf = []
        if step == 1:
            wf.append(("dg", th_state["agg_done"]))
        else:
            wf.append(("tv", th_state[step - 1]["tv_add"]))
        ln_rows(thought, tn_bf, 4, EPS_FFN, waits_first=wf)
        v_op(lambda e: e.tensor_mul(tn_bf, tn_bf, png4),
             waits=([("dg", v_png)] if step == 1 else ()))
        tv_tn = v_op(lambda e: e.tensor_add(tn_bf, tn_bf, pnb4), inc=True)
        ts_t = small_transposes(tn_bf, 16, tnT, 64, tv_tn)
        for osub in range(2):
            tt_l = stream_chain(
                16, lambda c0, n, osub=osub: d1_bfd[c0:c0 + n, :, osub * 512:(osub + 1) * 512]
                .rearrange("g p c -> p g c"),
                512, lambda c: tnT[:, c * 4:(c + 1) * 4],
                lambda osub=osub: d1b_row[:, osub * 512:(osub + 1) * 512],
                [("ts", ts_t)])
            ts_h = s_opd(lambda e, osub=osub: e.copy(
                tmp4[:, osub * 512:(osub + 1) * 512], PST[0:4, 0:512]),
                waits=[("tt", tt_l)])
            th_ps_guard[:] = [("ts", ts_h)]
        st["ts_g1"] = s_opd(lambda e: (e.drain(), e.activation(
            g1_bf, tmp4[:, 0:1024], AF.Gelu))[1])

    def step_part2(step):
        st = th_state[step]
        ts_gT = small_transposes(g1_bf, 8, g1T, 32, 0, extra=[("ts", st["ts_g1"])])
        ts_d = None
        for oc in range(4):
            tt_l = stream_chain(
                8, lambda c0, n, oc=oc: d2_bfd[c0:c0 + n, :, oc * 512:(oc + 1) * 512]
                .rearrange("g p c -> p g c"),
                512, lambda c: g1T[:, c * 4:(c + 1) * 4],
                lambda oc=oc: d2b8_row[:, oc * 512:(oc + 1) * 512],
                [("ts", ts_gT)])
            ts_d = s_opd(lambda e, oc=oc: e.copy(
                tmp4[:, oc * 512:(oc + 1) * 512], PST[0:4, 0:512]),
                waits=[("tt", tt_l)])
            th_ps_guard[:] = [("ts", ts_d)]
        P.dma("gpsimd", rs_ci, tmp4, waits=[("ts", ts_d)], sem="dg")
        st["c_n"] = coll("AllReduce", ALU.add, [list(range(8))], [rs_ci], [rs_co],
                         C_["dg"], "c_rs")

    def step_part3(step):
        st = th_state[step]
        v_back = P.dma("gpsimd", tmp4, rs_co, waits=[("c_rs", st["c_n"])], sem="dg")
        st["tv_add"] = v_op(lambda e: e.tensor_add(thought, thought, tmp4),
                            waits=[("dg", v_back)], inc=True)
        if debug:
            P.dma("gpsimd", dbg["dbg_steps"][:, step * 2048:(step + 1) * 2048],
                  thought, waits=[("tv", st["tv_add"])], sem="dg")

    def emit_bc_a():
        tv_c = v_op(lambda e: e.tensor_copy(tn_bf, thought),
                    waits=[("tv", th_state[4]["tv_add"])], inc=True)
        ts_t = small_transposes(tn_bf, 16, tnT, 64, tv_c)
        tt_l = stream_chain(
            16, lambda c0, n: bc_bfd[c0:c0 + n].rearrange("g p c -> p g c"), AGN,
            lambda c: tnT[:, c * 4:(c + 1) * 4], lambda: bcb_row, [("ts", ts_t)])
        tse = s_op(lambda e: e.copy(tmp4[:, 0:AGN], PST[0:4, 0:AGN]),
                   waits=[("tt", tt_l)], inc=True)
        th_ps_guard[:] = [("ts", tse)]
        P.dma("gpsimd", bc_ci, tmp4[:, 0:AGN], waits=[("ts", tse)], sem="dg")
        coll("AllGather", ALU.bypass, [list(range(8))], [bc_ci], [bc_co],
             C_["dg"], "c_bc")

    def emit_bc_b():
        for r in range(8):
            P.dma("gpsimd", thought[:, r * AGN:(r + 1) * AGN],
                  bc_co[4 * r:4 * r + 4, :],
                  waits=([("c_bc", 1)] if r == 0 else ()), sem="dg")
        return C_["dg"]

    v_agg_done = emit_agg()
    if debug:
        P.dma("gpsimd", dbg["dbg_steps"][:, 0:0 + 2048], thought,
              waits=[("dg", v_agg_done)], sem="dg")
        P.dma("gpsimd", dbg["dbg_xl"], xlT[:, 0:128],
              waits=[("ts", ts_xlt)], sem="dg")
    SCHED = {
        (1, 4): lambda: step_part1(1), (1, 12): lambda: step_part2(1),
        (2, 2): lambda: step_part3(1),
        (2, 6): lambda: step_part1(2), (2, 14): lambda: step_part2(2),
        (3, 2): lambda: step_part3(2),
        (3, 6): lambda: step_part1(3), (3, 14): lambda: step_part2(3),
        (4, 2): lambda: step_part3(3),
        (4, 6): lambda: step_part1(4), (4, 14): lambda: step_part2(4),
        (5, 2): lambda: step_part3(4),
        (5, 6): lambda: emit_bc_a(),
        (6, 2): lambda: emit_bc_b(),
    }
    v_bc_done = None

    for pos in range(NPOS):
        wc, chunks, mops, nch, ks, sink = s_layout(pos)
        slot = mk0 if pos % 2 == 0 else mk1

        def plan_S(it, h):
            kh = h // 4
            wS = [("ts", it_exp[it - 1])] if it > 0 else [("ts", ts_kvdone)]
            if h == 0:
                wS.append(("dq", mk_dma[pos]))
            def mk_S(e, pos=pos, h=h, kh=kh, chunks=chunks, sink=sink):
                last = None
                for (c0, n, kc0) in chunks:
                    last = e.matmul(PS0[:, c0:c0 + n],
                                    A1[:, h * 1024 + pos * 128: h * 1024 + (pos + 1) * 128],
                                    KT[:, kh * 2048 + kc0: kh * 2048 + kc0 + n],
                                    start=True, stop=True)
                if sink:
                    last = e.matmul(PS0[:, 1280:1284],
                                    A1[:, h * 1024 + pos * 128: h * 1024 + (pos + 1) * 128],
                                    KT[:, kh * 2048: kh * 2048 + 4],
                                    start=True, stop=True)
                return last
            it_S[it] = t_op(mk_S, waits=wS, inc=True)

        def plan_masks(it):
            def mk_M(e, mops=mops, slot=slot, sink=sink):
                last = None
                for (soff, w_, coff) in mops:
                    last = e.tensor_add(PS0[:, coff:coff + w_], PS0[:, coff:coff + w_],
                                        slot[:, soff:soff + w_])
                if sink:
                    last = e.tensor_add(PS0[:, 1280:1284], PS0[:, 1280:1284],
                                        slot[:, 512:516])
                return last
            it_mask[it] = v_op(mk_M, waits=[("tt", it_S[it])], inc=True)

        def plan_exp_scale(it):
            wE = [("tv", it_mask[it])]
            if it >= 1:
                wE.append(("tt", it_PT[it - 1]))
            def mk_E(e, wc=wc):
                return e.activation(P_sb[:, 0:wc], PS0[:, 0:wc], AF.Exp,
                                    accum_out=rowsum[:, 0:1])
            it_exp[it] = s_op(mk_E, waits=wE, inc=True)
            def mk_R(e, wc=wc):
                e.reciprocal(recip[:, 0:1], rowsum[:, 0:1])
                e.drain()
                return e.tensor_scalar_mul(P_sb[:, 0:wc], P_sb[:, 0:wc],
                                           recip[:, 0:1])
            it_scale[it] = v_op(mk_R, waits=[("ts", it_exp[it])], inc=True)

        def plan_ptav(it, h):
            kh = h // 4
            wPT = [("tv", it_scale[it])] + list(pt_guard)
            pt_guard[:] = []
            if it >= 1:
                wPT.append(("tv", it_evac[it - 1]))
            def mk_PT(e, nch=nch, sink=sink):
                last = None
                for cch in range(nch):
                    last = e.transpose(PSPT[:, cch * 128:(cch + 1) * 128],
                                       P_sb[:, cch * 128:(cch + 1) * 128],
                                       eye_sb[:, 0:128])
                if sink:
                    last = e.transpose(PSPT[0:4, 1408:1536],
                                       P_sb[:, 1280:1284], eye_sb[:, 0:128])
                return last
            it_PT[it] = t_op(mk_PT, waits=wPT, inc=True)
            ptb = PT0 if it % 2 == 0 else PT1
            wEV = [("tt", it_PT[it])]
            if it >= 2:
                wEV.append(("tt", it_AV[it - 2]))
            def mk_EV(e, ptb=ptb, nch=nch, sink=sink):
                last = e.tensor_copy(ptb[:, 0:nch * 128], PSPT[:, 0:nch * 128])
                if sink:
                    last = e.tensor_copy(ptb[0:4, 1408:1536], PSPT[0:4, 1408:1536])
                return last
            it_evac[it] = v_op(mk_EV, waits=wEV, inc=True)
            yslot = h % 8
            wAV = [("tv", it_evac[it])]
            if it >= 8:
                wAV.append(("ts", ts_yev[(it - 8) // 4]))
            def mk_AV(e, ptb=ptb, yslot=yslot, kh=kh, nch=nch, sink=sink, ks=ks):
                last = None
                for cch in range(nch):
                    g = ks + cch
                    last = e.matmul(PSY[:, yslot * 128:(yslot + 1) * 128],
                                    Vt[:, g * 512 + kh * 128: g * 512 + (kh + 1) * 128],
                                    ptb[:, cch * 128:(cch + 1) * 128],
                                    start=(cch == 0), stop=(cch == nch - 1 and not sink))
                if sink:
                    last = e.matmul(PSY[:, yslot * 128:(yslot + 1) * 128],
                                    Vt[0:4, kh * 128:(kh + 1) * 128],
                                    ptb[0:4, 1408:1536],
                                    start=False, stop=True)
                return last
            it_AV[it] = t_op(mk_AV, waits=wAV, inc=True)
            if h % 4 == 3:
                def mk_Y(e, pos=pos, h=h):
                    g4 = (h // 4) % 2
                    return e.copy(
                        YT[:, pos * 2048 + (h - 3) * 128: pos * 2048 + (h + 1) * 128],
                        PSY[:, g4 * 512:(g4 + 1) * 512])
                ts_yev[it // 4] = s_op(mk_Y, waits=[("tt", it_AV[it])], inc=True)

        for h in range(NH):
            it = pos * NH + h
            plan_S(it, h)
            plan_masks(it)
            if h >= 1:
                plan_ptav(it - 1, h - 1)
            plan_exp_scale(it)
            if (pos, h) in SCHED:
                r_ = SCHED[(pos, h)]()
                if (pos, h) == (6, 2):
                    v_bc_done = r_
        plan_ptav(pos * NH + NH - 1, NH - 1)
        if pos + 2 < NPOS:
            plan_mask_dma(pos + 2, [("tv", it_mask[pos * NH + NH - 1])])
    ts_attn = C_["ts"]
    tt_attn = C_["tt"]

    if debug:
        P.dma("gpsimd", dbg["dbg_yT"], YT[:, :],
              waits=[("ts", ts_yev[NPOS * NH // 4 - 1])], sem="dg")

    tv_bcc = v_op(lambda e: e.tensor_copy(tn_bf, thought),
                  waits=[("dg", v_bc_done)], inc=True)
    ts_prev = ts_attn
    bcast_jobs = [(tn_bf, bc_bcast, 0, "b1"), (tn_bf, bc_bcast, 1, "b1"),
                  (outg_row, g_bcast, 0, "p64"), (outg_row, g_bcast, 1, "p64"),
                  (outb_row, b_bcast, 0, "p0"), (outb_row, b_bcast, 1, "p0")]
    for srcrow, dstt, half, mode in bcast_jobs:
        w = [("ts", ts_prev)]
        if mode == "b1":
            w.append(("tv", tv_bcc))
        def mm(e, srcrow=srcrow, half=half, mode=mode):
            if mode == "b1":
                e.matmul(PS0[:, 0:512], b1_sb,
                         srcrow[:, half * 1024:half * 1024 + 512],
                         start=True, stop=True)
                return e.matmul(PS0[:, 512:1024], b1_sb,
                                srcrow[:, half * 1024 + 512:half * 1024 + 1024],
                                start=True, stop=True)
            lhs = ones_sb[64:65, 0:128] if mode == "p64" else ones_sb[0:1, 0:128]
            e.matmul(PS0[:, 0:512], lhs,
                     srcrow[:, half * 1024:half * 1024 + 512],
                     start=True, stop=True)
            return e.matmul(PS0[:, 512:1024], lhs,
                            srcrow[:, half * 1024 + 512:half * 1024 + 1024],
                            start=True, stop=True)
        tt_g = t_op(mm, waits=w, inc=True)
        ts_prev = s_op(lambda e, dstt=dstt, half=half: e.copy(
            dstt[:, half * 1024:(half + 1) * 1024], PS0[:, 0:1024]),
            waits=[("tt", tt_g)], inc=True)
    ts_bcast = ts_prev

    if debug:
        P.dma("gpsimd", dbg["dbg_bc"], bc_bcast, waits=[("ts", ts_bcast)], sem="dg")
        P.dma("gpsimd", dbg["dbg_th"], thought, waits=[("ts", ts_bcast)], sem="dg")

    if upto <= 3:
        return nc, P, dbg
    tt_wo = []
    ts_sv = []
    for j in range(8):
        buf = wbufa if j % 2 == 0 else wbufb
        w = ([("tt", tt_wo[j - 2])] if j >= 2 else
             (list(th_stream_guard) if j == 0 else [("tt", tt_mb[-1])]))
        vw = P.dma("gpsimd", buf, wo_t[j], waits=w, sem="dg")
        for pos in range(NPOS):
            jp = j * NPOS + pos
            bank = jp % 2
            wmm = [("dg", vw)] if pos == 0 else []
            if jp >= 2:
                wmm.append(("ts", ts_sv[jp - 2]))
            if jp == 0:
                wmm.append(("ts", ts_bcast))
            def mk_wo(e, j=j, pos=pos, buf=buf, bank=bank):
                last = None
                for hh in range(16):
                    last = e.matmul(
                        PSY[:, bank * 512:bank * 512 + 256],
                        YT[:, pos * 2048 + hh * 128: pos * 2048 + (hh + 1) * 128],
                        buf[:, hh * 256:(hh + 1) * 256],
                        start=(hh == 0), stop=(hh == 15))
                return last
            tt_l = t_op(mk_wo, waits=wmm, inc=True)
            ts_sv.append(s_op(lambda e, j=j, pos=pos, bank=bank: e.copy(
                A1[:, pos * 2048 + j * 256: pos * 2048 + (j + 1) * 256],
                PSY[:, bank * 512:bank * 512 + 256]),
                waits=[("tt", tt_l)], inc=True))
        tt_wo.append(tt_l)
    ts_wo_done = ts_sv[-1]

    if debug:
        P.dma("gpsimd", dbg["dbg_s"], A1[:, :], waits=[("ts", ts_wo_done)], sem="dg")

    if upto <= 4:
        return nc, P, dbg
    tt_gw = []
    ts_gv = []
    tv_u = []
    tg_u = []
    for j in range(8):
        buf = wbufa if j % 2 == 0 else wbufb
        w = [("tt", tt_gw[j - 2])] if j >= 2 else [("tt", tt_wo[6 + j])]
        vw = P.dma("gpsimd", buf, gw_t[j], waits=w, sem="dg")
        for pos in range(NPOS):
            jp = j * NPOS + pos
            bank = jp % 2
            wmm = [("dg", vw)] if pos == 0 else []
            if jp == 0:
                wmm.append(("ts", ts_wo_done))
            if jp >= 2:
                wmm.append(("ts", ts_gv[jp - 2]))
            def mk_g(e, j=j, pos=pos, buf=buf, bank=bank):
                last = None
                for cch in range(16):
                    last = e.matmul(
                        PSY[:, bank * 512:bank * 512 + 256],
                        xnT[:, cch * 1024 + pos * 128: cch * 1024 + (pos + 1) * 128],
                        buf[:, cch * 256:(cch + 1) * 256],
                        start=(cch == 0), stop=False)
                return e.matmul(PSY[:, bank * 512:bank * 512 + 256],
                                ones_sb[32:33, 0:128],
                                gateb_row[:, j * 256:(j + 1) * 256],
                                start=False, stop=True)
            tt_l = t_op(mk_g, waits=wmm, inc=True)
            slc = (jp % 2) * 1024 + ((jp // 2) % 2) * 256
            wtr = []
            if jp >= 2:
                wtr.append(("tv", tv_u[jp - 2]))
            if jp == 0:
                wtr.extend(pt_guard)
                pt_guard[:] = []
                wtr.append(("tv", it_evac[NPOS * NH - 1]))
            def mk_xtr(e, j=j, pos=pos, slc=slc):
                e.transpose(PSPT[:, slc:slc + 128],
                            xnT[:, (2 * j) * 1024 + pos * 128:(2 * j) * 1024 + (pos + 1) * 128],
                            eye_sb[:, 0:128])
                return e.transpose(PSPT[:, slc + 128:slc + 256],
                                   xnT[:, (2 * j + 1) * 1024 + pos * 128:(2 * j + 1) * 1024 + (pos + 1) * 128],
                                   eye_sb[:, 0:128])
            tt_x = t_op(mk_xtr, waits=wtr, inc=True)
            gb = gtmpa if jp % 2 == 0 else gtmpb
            wsg = [("tt", tt_l)]
            if jp >= 2:
                wsg.append(("tg", tg_u[jp - 2]))
            ts_gv.append(s_op(lambda e, gb=gb, bank=bank: e.activation(
                gb[:, 0:256], PSY[:, bank * 512:bank * 512 + 256], AF.Sigmoid),
                waits=wsg, inc=True))
            ut = utmpa if jp % 2 == 0 else utmpb
            wut = [("ts", ts_gv[-1])]
            if jp == 0:
                wut.append(("ts", ts_bcast))
            if jp >= 2:
                wut.append(("tv", tv_u[jp - 2]))
            def mk_ut(e, gb=gb, ut=ut, j=j):
                return e.tensor_mul(ut[:, 0:256], gb[:, 0:256],
                                    bc_bcast[:, j * 256:(j + 1) * 256])
            P.op("gpsimd", mk_ut, waits=wut, incs=(("tg", 1),))
            tg_u.append(C_["tg"])
            piece = A2[:, pos * 2048 + j * 256: pos * 2048 + (j + 1) * 256]
            tv_u.append(v_op(lambda e, ut=ut, slc=slc, piece=piece: e.tensor_add(
                piece, ut[:, 0:256], PSPT[:, slc:slc + 256]),
                waits=[("tg", tg_u[-1]), ("tt", tt_x)], inc=True))
            ts_p1_ = s_op(lambda e, piece=piece: e.activation(
                sqb.bitcast(BF16)[:, 0:256], piece, AF.Identity,
                accum_out=pacc1[:, 0:1]), waits=[("tv", tv_u[-1])])
            ts_p2_ = s_op(lambda e, piece=piece: e.activation(
                sqb.bitcast(BF16)[:, 0:256], piece, AF.Square,
                accum_out=pacc2[:, 0:1]), inc=True)
            if j == 0:
                v_op(lambda e, pos=pos: e.tensor_copy(
                    usum[:, pos:pos + 1], pacc1[:, 0:1]),
                    waits=[("ts", ts_p2_)])
                v_op(lambda e, pos=pos: e.tensor_copy(
                    usq[:, pos:pos + 1], pacc2[:, 0:1]), inc=True)
            else:
                v_op(lambda e, pos=pos: e.tensor_add(
                    usum[:, pos:pos + 1], usum[:, pos:pos + 1], pacc1[:, 0:1]),
                    waits=[("ts", ts_p2_)])
                v_op(lambda e, pos=pos: e.tensor_add(
                    usq[:, pos:pos + 1], usq[:, pos:pos + 1], pacc2[:, 0:1]),
                    inc=True)
            tv_u[-1] = C_["tv"]
        tt_gw.append(tt_l)
    tv_u_done = tv_u[-1]

    if debug:
        P.dma("gpsimd", dbg["dbg_u"], A2[:, :], waits=[("tv", tv_u_done)], sem="dg")
        pass

    if upto <= 5:
        return nc, P, dbg
    for pos in range(NPOS):
        w0 = [("tv", tv_u_done)] if pos == 0 else ()
        v_op(lambda e, pos=pos: e.tensor_scalar(
            urs[:, pos:pos + 1], usq[:, pos:pos + 1], 1.0 / 2048.0, EPS_FFN,
            ALU.mult, ALU.add), waits=w0)
        v_op(lambda e, pos=pos: e.tensor_scalar_mul(
            une[:, pos:pos + 1], usum[:, pos:pos + 1], -1.0 / 2048.0))
        v_op(lambda e: e.drain())
        v_op(lambda e, pos=pos: e.tensor_mul(
            bag[:, 0:1], une[:, pos:pos + 1], une[:, pos:pos + 1]))
        v_op(lambda e: e.drain())
        v_op(lambda e, pos=pos: e.tensor_sub(
            urs[:, pos:pos + 1], urs[:, pos:pos + 1], bag[:, 0:1]))
    tv_ag = v_op(lambda e: e.drain(), inc=True)
    if debug:
        P.dma("gpsimd", dbg["dbg_us"][:, 0:8], usum[:, 0:8],
              waits=[("tv", tv_ag)], sem="dg")
        P.dma("gpsimd", dbg["dbg_us"][:, 8:16], usq[:, 0:8], sem="dg")
        P.dma("gpsimd", dbg["dbg_us"][:, 16:24], une[:, 0:8], sem="dg")
        P.dma("gpsimd", dbg["dbg_us"][:, 24:32], urs[:, 0:8], sem="dg")
    s_op(lambda e: (e.activation(urs[:, 0:NPOS], urs[:, 0:NPOS], AF.Ln),
                    e.drain())[0], waits=[("tv", tv_ag)])
    ts_rstd = s_op(lambda e: e.activation(urs[:, 0:NPOS], urs[:, 0:NPOS],
                                          AF.Exp, scale=-0.5), inc=True)
    ts_silu = {}
    tv_fin = {}
    dq_out = {}
    for pos in range(NPOS):
        xr = xre0 if pos % 2 == 0 else xre1
        wx = [("tv", tv_u_done)]
        if pos >= 2:
            wx.append(("dq", dq_out[pos - 2]))
        v_x = P.dma("sync", xr, x_own[pos * 128:(pos + 1) * 128, :], waits=wx)
        wv = [("ts", ts_rstd)]
        if pos >= 1:
            wv.append(("ts", ts_silu[pos - 1]))
        def mk_n(e, pos=pos):
            e.drain()
            return e.tensor_scalar(vt_f[:, 0:2048],
                                   A2[:, pos * 2048:(pos + 1) * 2048],
                                   une[:, pos:pos + 1], urs[:, pos:pos + 1],
                                   ALU.add, ALU.mult)
        v_op(mk_n, waits=wv)
        v_op(lambda e: e.tensor_mul(vt_f[:, 0:2048], vt_f[:, 0:2048], g_bcast))
        tv_v = v_op(lambda e: e.tensor_add(vt_f[:, 0:2048], vt_f[:, 0:2048], b_bcast),
                    inc=True)
        ts_silu[pos] = s_opd(lambda e: e.activation(vt_b[:, 0:2048], vt_f[:, 0:2048],
                                                    AF.Silu),
                             waits=[("tv", tv_v)])
        def mk_w(e, pos=pos):
            return e.tensor_mul(vt_f[:, 0:2048], vt_b[:, 0:2048],
                                A1[:, pos * 2048:(pos + 1) * 2048])
        v_op(mk_w, waits=[("ts", ts_silu[pos])])
        def mk_o(e, xr=xr, v_x=v_x):
            e.tensor_add(xr, xr, vt_f[:, 0:2048])
            return e.drain()
        tv_fin[pos] = v_op(mk_o, waits=[("dq", v_x)], inc=True)
        dq_out[pos] = P.dma("sync", out_own[pos * 128:(pos + 1) * 128, :], xr,
                            waits=[("tv", tv_fin[pos])])
    fin = C_["dq"]
    P.op("sync", lambda e: e.nop(), waits=[("dq", fin)])
    P.op("gpsimd", lambda e: e.nop(), waits=[("dq", fin), ("dg", C_["dg"])])

    return nc, P, dbg


def build_full(debug=False, upto=6):
    nc, P, dbg = build_program(debug=debug, upto=upto)
    with nc.Block() as block:
        @block.sync
        def _(e):
            P.replay("sync", e)

        @block.gpsimd
        def _(e):
            P.replay("gpsimd", e)

        @block.scalar
        def _(e):
            P.replay("scalar", e)

        @block.vector
        def _(e):
            P.replay("vector", e)

        @block.tensor
        def _(e):
            P.replay("tensor", e)
    return nc, dbg


def kernel(**inputs):
    x = np.asarray(inputs["L0_attn_x"])
    args = dict(
        Wq=np.asarray(inputs["L0_attn_Wq"]), Wk=np.asarray(inputs["L0_attn_Wk"]),
        Wv=np.asarray(inputs["L0_attn_Wv"]), Wo=np.asarray(inputs["L0_attn_Wo"]),
        in_g=np.asarray(inputs["L0_ffn_in_gamma"]), in_b=np.asarray(inputs["L0_ffn_in_beta"]),
        agg_w=np.asarray(inputs["L0_ffn_agg_w"]), agg_b=np.asarray(inputs["L0_ffn_agg_b"]),
        pn_g=np.asarray(inputs["L0_ffn_pn_gamma"]), pn_b=np.asarray(inputs["L0_ffn_pn_beta"]),
        d1_w=np.asarray(inputs["L0_ffn_d1_w"]), d1_b=np.asarray(inputs["L0_ffn_d1_b"]),
        d2_w=np.asarray(inputs["L0_ffn_d2_w"]), d2_b=np.asarray(inputs["L0_ffn_d2_b"]),
        bc_w=np.asarray(inputs["L0_ffn_bc_w"]), bc_b=np.asarray(inputs["L0_ffn_bc_b"]),
        gate_w=np.asarray(inputs["L0_ffn_gate_w"]), gate_b=np.asarray(inputs["L0_ffn_gate_b"]),
        out_g=np.asarray(inputs["L0_ffn_out_gamma"]), out_b=np.asarray(inputs["L0_ffn_out_beta"]),
    )
    shared = {
        "x_last": np.ascontiguousarray(x[:, -1, :]),
        "wq_t": _mblocks(np.ascontiguousarray(args["Wq"].T), 16),
        "wk_t": _mblocks(np.ascontiguousarray(args["Wk"].T), 4),
        "wv_t": _mblocks(np.ascontiguousarray(args["Wv"].T), 4),
        "wo_t": _osubs(np.ascontiguousarray(args["Wo"].T), 8, 256),
        "gw_t": _osubs(np.ascontiguousarray(args["gate_w"]), 8, 256),
        "gate_b": np.ascontiguousarray(args["gate_b"].reshape(1, C)),
        "d2_b": np.ascontiguousarray(args["d2_b"].reshape(1, C)),
        "out_g": np.ascontiguousarray(args["out_g"].reshape(1, C)),
        "out_be": np.ascontiguousarray(args["out_b"].reshape(1, C)),
        "in_g": np.ascontiguousarray(args["in_g"].reshape(1, C)),
        "in_b": np.ascontiguousarray(args["in_b"].reshape(1, C)),
        "pn_g": np.ascontiguousarray(args["pn_g"].reshape(1, C)),
        "pn_b": np.ascontiguousarray(args["pn_b"].reshape(1, C)),
    }
    in_maps = [make_core_inputs(c, x, shared=shared, **args) for c in range(8)]

    nc, _dbg = build_full(debug=False)
    res = run_bass_kernel_spmd(nc, in_maps, core_ids=list(range(8)))

    out = np.empty((4, 2048, C), np.float32)
    for c in range(8):
        b, par = c // 2, c % 2
        blocks = POS_BLOCKS[par]
        oc = res.results[c]["out_own"]
        for pos, g in enumerate(blocks):
            out[b, g * 128:(g + 1) * 128, :] = oc[pos * 128:(pos + 1) * 128, :]
    return out
